# version 02
# baseline (speedup 1.0000x reference)
from contextlib import ExitStack

import numpy as np
import ml_dtypes
import concourse.bass as bass
import concourse.mybir as mybir
from concourse.bass_utils import run_bass_kernel_spmd

F32 = mybir.dt.float32
BF16 = mybir.dt.bfloat16
AF = mybir.ActivationFunctionType
ALU = mybir.AluOpType
AX = mybir.AxisListType
NPBF = np.dtype(ml_dtypes.bfloat16)

NCORES = 8
BATCH, SEQ, DM = 4, 4096, 4096
T = 2048
TH = T + 3
NH, HD, SS, NG = 128, 64, 128, 8
INTER = NH * HD
CONV = INTER + 2 * NG * SS
PROJ = INTER + CONV + NH
CHUNK = 128
NCH = T // CHUNK
EPS = 1e-5
GROUP = INTER // NG
FH = 11264
FQ = 2816

ENGS = ("sync", "tensor", "vector", "scalar", "gpsimd")


class Prog:

    def __init__(self):
        self.ops = {e: [] for e in ENGS}
        self.cnt = {}
        self.dma_sems = set()

    def sem(self, *names):
        for n in names:
            self.cnt[n] = 0

    def on(self, eng, fn=None, wait=(), inc=()):
        for s, n in inc:
            self.cnt[s] += n
        after = {s: self.cnt[s] for s, _ in inc}
        self.ops[eng].append((fn, list(wait), list(inc)))
        return after

    def v(self, s):
        return self.cnt[s]

    def stream(self, name, eng="sync", nbuf=2):
        return Stream(self, name, eng, nbuf)

    def barrier_all(self):
        for e in ENGS:
            self.on(e, lambda en: en.nop(), inc=[("bar", 1)])
        target = self.v("bar")
        qw = [(s, self.v(s)) for s in self.cnt
              if s.startswith("q") and self.v(s) > 0]
        for e in ENGS:
            self.on(e, None, wait=[("bar", target)] + qw)

    def quiesce(self):
        w = [(s, self.v(s)) for s in self.dma_sems if self.v(s) > 0]
        return self.on("sync", lambda e: e.nop(), wait=w,
                       inc=[("bar", 1)])["bar"]

    def dma(self, eng, out, in_, sem, wait=(), **kw):
        self.dma_sems.add(sem)
        return self.on(eng, lambda e: e.dma_start(out=out, in_=in_, **kw),
                       wait=wait, inc=[(sem, 16)])[sem]

    def install(self, block, sems):
        for eng_name in ENGS:
            ops = self.ops[eng_name]
            if not ops:
                continue

            def body(engine, ops=ops):
                last_wait = {}
                for fn, wait, inc in ops:
                    for s, val in wait:
                        if last_wait.get(s, -1) >= val:
                            continue
                        engine.wait_ge(sems[s], val)
                        last_wait[s] = val
                    inst = fn(engine) if fn is not None else None
                    if inc:
                        assert inst is not None, "inc requires an instruction"
                        for s, n in inc:
                            inst.then_inc(sems[s], n)
            getattr(block, eng_name)(body)


class Stream:

    def __init__(self, p, name, eng="sync", nbuf=2):
        self.p, self.name, self.eng, self.nbuf = p, name, eng, nbuf
        p.sem(*[f"{name}{i}" for i in range(nbuf)])
        self.i = 0

    def load(self, out, in_, wait=(), **kw):
        s = f"{self.name}{self.i % self.nbuf}"
        v = self.p.dma(self.eng, out, in_, s, wait=wait, **kw)
        self.i += 1
        return (s, v)

    def sems(self):
        return [f"{self.name}{i}" for i in range(self.nbuf)]


def apv(t, offset, dims):
    if isinstance(t, bass.AP):
        return bass.AP(t.tensor, offset, [list(d) for d in dims])
    a = t.ap()
    return bass.AP(a.tensor, offset, [list(d) for d in dims])


def rowstep(t):
    return t.ap().ap[0][0]



def emit_phase1(p, io, sb, ps):
    x0c, WinT = io["x0c"], io["WinT"]
    gate_sp, xtm_sp, bc_sp, dt_sp = io["gate_sp"], io["xtm_sp"], io["bc_sp"], io["dt_sp"]

    h1f = sb["h1f"]
    xld = sb["xld"]
    h1t = sb["h1t"]
    wta = sb["wta"]
    xcb = sb["xcb"]
    cacc = sb["cacc"]
    cacd = sb["cacd"]
    xcv = sb["xcv"]
    xT = sb["xT"]
    cwsb, cbsb = sb["cwsb"], sb["cbsb"]
    dtb = sb["dtb"]
    idbf = sb["idbf"]
    cols = sb["cols"]
    pmm = ps["pmm"]
    ptr = ps["ptr"]

    HROW = rowstep(h1f)
    WROW = rowstep(wta)
    XROW = rowstep(xT)

    p.dma("sync", sb["cwsb"][:, :], io["cw_d"], "qs")
    p.dma("sync", sb["cbsb"][:, :], io["cb_d"], "qs")
    p.dma("sync", sb["dtb"][:, :], io["dtb_d"], "qs")
    qs_setup = p.dma("sync", sb["idbf"][:, :], io["idbf_d"], "qs")
    bar0 = p.quiesce()

    dve_h1 = {}
    tb_dve = {}
    tb = 0
    pe_trans_last = None
    st_x = p.stream("x1l", "sync", 1)
    st_w = p.stream("w1t", "gpsimd", 2)
    st_xT = p.stream("s1x", "sync", 1)
    st_bc = p.stream("s1b", "sync", 1)
    for tt in range(17):
        rows = 128 if tt < 16 else 3
        xv = xld[0:rows, 0:4096]
        ww = [("dve", dve_h1[tt - 1])] if tt - 1 in dve_h1 else []
        qs_x = st_x.load(xv, x0c[tt * 128: tt * 128 + rows, :],
                         wait=ww + [("bar", bar0)])
        (c_sum, c_mean, c_sqa, c_sqb, c_sq, c_m2, c_var, c_rstd) = (
            cols[0:rows, i:i + 1] for i in range(8))

        def a_sq(e, xv=xv, c_sqa=c_sqa, c_sqb=c_sqb, rows=rows):
            e.activation(cacc[0:rows, 0:2048], xv[:, 0:2048], AF.Square,
                         accum_out=c_sqa)
            return e.activation(cacd[0:rows, 0:2048], xv[:, 2048:4096], AF.Square,
                                accum_out=c_sqb)
        act_sq = p.on("scalar", a_sq, wait=[qs_x], inc=[("act", 1)])["act"]

        def v_sum(e, xv=xv, c_sum=c_sum):
            return e.tensor_reduce(c_sum, xv, AX.X, ALU.add)
        da = p.on("vector", v_sum, wait=[qs_x], inc=[("dve", 1)])["dve"]
        db = p.on("vector", lambda e, a=c_mean, bb=c_sum: e.tensor_scalar_mul(
            a, bb, 1.0 / DM), wait=[("dve", da)], inc=[("dve", 1)])["dve"]
        dc = p.on("vector", lambda e, a=c_m2, bb=c_mean: e.tensor_tensor(
            a, bb, bb, ALU.mult), wait=[("dve", db)], inc=[("dve", 1)])["dve"]
        dd = p.on("vector", lambda e, a=c_sq, x1=c_sqa, x2=c_sqb: e.tensor_tensor(
            a, x1, x2, ALU.add), wait=[("act", act_sq)], inc=[("dve", 1)])["dve"]
        de = p.on("vector", lambda e, a=c_var, bb=c_sq: e.tensor_scalar(
            a, bb, 1.0 / DM, EPS, ALU.mult, ALU.add),
            wait=[("dve", dd)], inc=[("dve", 1)])["dve"]
        dve_st = p.on("vector", lambda e, a=c_var, bb=c_m2: e.tensor_tensor(
            a, a, bb, ALU.subtract), wait=[("dve", de)], inc=[("dve", 1)])["dve"]

        a_ln = p.on("scalar", lambda e, a=c_var: e.activation(a, a, AF.Ln),
                    wait=[("dve", dve_st)], inc=[("act", 1)])["act"]
        act_r = p.on("scalar", lambda e, a=c_rstd, bb=c_var: e.activation(
            a, bb, AF.Exp, scale=-0.5), wait=[("act", a_ln)],
            inc=[("act", 1)])["act"]

        wh = [("pe", pe_trans_last)] if pe_trans_last is not None else []
        dve_h1[tt] = p.on("vector", lambda e, xv=xv, rows=rows, m=c_mean, r=c_rstd:
                          e.tensor_scalar(h1t[0:rows, :], xv, m, r,
                                          ALU.subtract, ALU.mult),
                          wait=[("act", act_r)] + wh, inc=[("dve", 1)])["dve"]

        for bt in range(4):
            pb = tb % 2
            wtb = [("act", tb_dve[tb - 2])] if tb - 2 in tb_dve else []

            def f_tr(e, bt=bt, rows=rows, pb=pb):
                last = None
                for i in range(8):
                    kt = bt * 8 + i
                    last = e.transpose(ptr[pb][:, i * 128: i * 128 + rows],
                                       h1t[0:rows, kt * 128:(kt + 1) * 128],
                                       idbf[0:rows, 0:rows])
                return last
            pe_tr = p.on("tensor", f_tr, wait=[("dve", dve_h1[tt])] + wtb,
                         inc=[("pe", 1)])["pe"]
            pe_trans_last = pe_tr

            def f_ev(e, bt=bt, rows=rows, pb=pb, tt=tt):
                src = apv(ptr[pb], 0, [[1024, 128], [128, 8], [1, rows]])
                dst = apv(h1f, (bt * 8) * 2051 + tt * 128,
                          [[HROW, 128], [2051, 8], [1, rows]])
                return e.copy(dst, src)
            tb_dve[tb] = p.on("scalar", f_ev, wait=[("pe", pe_tr)],
                              inc=[("act", 1)])["act"]
            tb += 1

    stepA_done = tb_dve[tb - 1]
    if "h1dbg" in io:
        p.dma("sync", io["h1dbg"], h1f[:, :], "qs", wait=[("act", stepA_done)])

    NS = [(0, 512), (512, 512), (1024, 512), (1536, 512), (2048, 3)]
    pe_mm = {}
    act_ev = {}
    xcb_free = {}
    conv_free = {}
    u = 0
    last_xT = None
    last_dve_c = None
    last_act_s = None

    for jb in range(145):
        b = jb % 2
        ww = [("pe", pe_mm[(jb - 2, 4)])] if (jb - 2, 4) in pe_mm else []
        src = apv(WinT, jb * 128, [[PROJ, 128], [128 * PROJ, 32], [1, 128]])
        dst = apv(wta, b * 4096, [[WROW, 128], [128, 32], [1, 128]])
        qg_w = st_w.load(dst, src, wait=ww)

        for ns in range(5):
            n0, nn = NS[ns]
            bank = u % 4
            wb = [("act", act_ev[u - 4])] if u - 4 in act_ev else []

            def f_mm(e, b=b, n0=n0, nn=nn, bank=bank):
                last = None
                for kt in range(32):
                    last = e.matmul(
                        pmm[bank][:, 0:nn],
                        wta[:, b * 4096 + kt * 128: b * 4096 + (kt + 1) * 128],
                        h1f[:, kt * 2051 + n0: kt * 2051 + n0 + nn],
                        start=(kt == 0), stop=(kt == 31))
                return last
            pe_mm[(jb, ns)] = p.on("tensor", f_mm,
                                   wait=[qg_w, ("act", stepA_done)] + wb,
                                   inc=[("pe", 1)])["pe"]
            if jb < 144:
                wx = xcb_free.get(jb - 2, [])

                def f_ev(e, b=b, n0=n0, nn=nn, bank=bank):
                    return e.copy(xcb[:, b * 2051 + n0: b * 2051 + n0 + nn],
                                  pmm[bank][:, 0:nn])
            else:
                wx = [("dve", last_dve_c)] if last_dve_c is not None else []

                def f_ev(e, n0=n0, nn=nn, bank=bank):
                    return e.copy(cacc[:, n0:n0 + nn], pmm[bank][:, 0:nn])
            act_ev[u] = p.on("scalar", f_ev,
                             wait=[("pe", pe_mm[(jb, ns)])] + wx,
                             inc=[("act", 1)])["act"]
            u += 1
        evac_all = act_ev[u - 1]

        def spill_transposed(src_sb, src_off, dst_dram, dst_col, wait_pre):
            nonlocal tb, pe_trans_last, last_xT
            pe_last = None
            for bt in range(2):
                pb = tb % 2
                wtb = [("act", tb_dve[tb - 2])] if tb - 2 in tb_dve else []

                def f_tr(e, bt=bt, pb=pb, src_sb=src_sb, src_off=src_off):
                    last = None
                    for i in range(8):
                        ti = bt * 8 + i
                        last = e.transpose(
                            ptr[pb][:, i * 128:(i + 1) * 128],
                            src_sb[:, src_off + ti * 128: src_off + (ti + 1) * 128],
                            idbf[:, :])
                    return last
                pe_tr = p.on("tensor", f_tr, wait=wait_pre + wtb,
                             inc=[("pe", 1)])["pe"]
                pe_last = pe_tr
                wxT = [last_xT] if last_xT is not None else []

                def f_ev(e, bt=bt, pb=pb):
                    src = apv(ptr[pb], 0, [[1024, 128], [128, 8], [1, 128]])
                    dst = apv(xT, bt * 1024, [[XROW, 128], [128, 8], [1, 128]])
                    return e.copy(dst, src)
                tb_dve[tb] = p.on("scalar", f_ev, wait=[("pe", pe_tr)] + wxT,
                                  inc=[("act", 1)])["act"]
                tb += 1
            src = apv(xT, 0, [[XROW, 128], [128, 16], [1, 128]])
            dst = apv(dst_dram, dst_col,
                      [[INTER, 128], [128 * INTER, 16], [1, 128]])
            last_xT = st_xT.load(dst, src, wait=[("act", tb_dve[tb - 1])])
            pe_trans_last = pe_last
            return pe_last

        if jb < 64:
            pe_last = spill_transposed(xcb, b * 2051 + 3, gate_sp, jb * 128,
                                       [("act", evac_all)])
            xcb_free[jb] = [("pe", pe_last)]
        elif jb < 144:
            cj = jb - 64
            w_cacd = [("act", last_act_s)] if last_act_s is not None else []

            def f_cd(e, b=b, cj=cj):
                e.tensor_scalar(cacd[:, 0:2048],
                                xcb[:, b * 2051: b * 2051 + 2048],
                                cwsb[:, cj * 4: cj * 4 + 1], None, ALU.mult)
                inst = None
                for k in (1, 2, 3):
                    inst = e.scalar_tensor_tensor(
                        cacd[:, 0:2048],
                        xcb[:, b * 2051 + k: b * 2051 + k + 2048],
                        cwsb[:, cj * 4 + k: cj * 4 + k + 1],
                        cacd[:, 0:2048],
                        ALU.mult, ALU.add)
                return inst
            dve_c = p.on("vector", f_cd, wait=[("act", evac_all)] + w_cacd,
                         inc=[("dve", 1)])["dve"]
            last_dve_c = dve_c

            w_xcv = conv_free.get(jb - 1, [])

            def f_si(e, cj=cj):
                return e.activation(xcv[:, 0:2048], cacd[:, 0:2048],
                                    AF.Silu, bias=cbsb[:, cj:cj + 1])
            act_s = p.on("scalar", f_si, wait=[("dve", dve_c)] + w_xcv,
                         inc=[("act", 1)])["act"]
            last_act_s = act_s
            xcb_free[jb] = [("dve", dve_c)]

            if jb < 128:
                pe_last = spill_transposed(xcv, 0, xtm_sp, (jb - 64) * 128,
                                           [("act", act_s)])
                conv_free[jb] = [("pe", pe_last)]
            else:
                qd = st_bc.load(bc_sp[(jb - 128) * 128:(jb - 127) * 128, :],
                                xcv[:, 0:2048], wait=[("act", act_s)])
                conv_free[jb] = [qd]
        else:
            raw = cacc[:, 0:2051]
            aa = cacd[:, 0:2051]
            tt_ = xld[:, 0:2051]
            w_sil = [("act", last_act_s)] if last_act_s is not None else []
            d0 = p.on("vector", lambda e: e.tensor_scalar(
                raw, raw, dtb[:, 0:1], None, ALU.add), wait=[("act", evac_all)],
                inc=[("dve", 1)])["dve"]
            a_abs = p.on("scalar", lambda e: e.activation(aa, raw, AF.Abs),
                         wait=[("dve", d0)] + w_sil, inc=[("act", 1)])["act"]
            a0 = p.on("scalar", lambda e: e.activation(tt_, aa, AF.Exp, scale=-1.0),
                      inc=[("act", 1)])["act"]
            d2 = p.on("vector", lambda e: e.tensor_scalar_add(tt_, tt_, 1.0),
                      wait=[("act", a0)], inc=[("dve", 1)])["dve"]
            a1 = p.on("scalar", lambda e: e.activation(aa, tt_, AF.Ln),
                      wait=[("dve", d2)], inc=[("act", 1)])["act"]
            p.on("vector", lambda e: e.tensor_scalar(
                tt_, raw, 0.0, None, ALU.max), wait=[("act", a1)])
            p.on("vector", lambda e: e.tensor_tensor(raw, tt_, aa, ALU.add))
            d3 = p.on("vector", lambda e: e.tensor_scalar(
                tt_, raw, 0.001, 100.0, ALU.max, ALU.min), inc=[("dve", 1)])["dve"]
            p.dma("sync", dt_sp[:, :], tt_[:, 3:2051], "qs", wait=[("dve", d3)])

    p.on("sync", None, wait=[(s, p.v(s)) for s in p.dma_sems if p.v(s) > 0])


def build_phase1_io(nc, internal=False):
    kin = "Internal" if internal else "ExternalInput"
    kout = "Internal" if internal else "ExternalOutput"
    io = {}
    io["x0c"] = nc.dram_tensor("x0c", [TH, DM], F32, kind="ExternalInput").ap()
    io["WinT"] = nc.dram_tensor("WinT", [DM, PROJ], F32, kind="ExternalInput").ap()
    io["cw_d"] = nc.dram_tensor("cw_d", [128, 320], F32, kind="ExternalInput").ap()
    io["cb_d"] = nc.dram_tensor("cb_d", [128, 80], F32, kind="ExternalInput").ap()
    io["dtb_d"] = nc.dram_tensor("dtb_d", [128, 1], F32, kind="ExternalInput").ap()
    io["idbf_d"] = nc.dram_tensor("idbf_d", [128, 128], BF16, kind="ExternalInput").ap()
    io["gate_sp"] = nc.dram_tensor("gate_sp", [T, INTER], BF16, kind=kout).ap()
    io["xtm_sp"] = nc.dram_tensor("xtm_sp", [T, INTER], BF16, kind=kout).ap()
    io["bc_sp"] = nc.dram_tensor("bc_sp", [2048, T], BF16, kind=kout).ap()
    io["dt_sp"] = nc.dram_tensor("dt_sp", [128, T], F32, kind=kout).ap()
    return io


def alloc_phase1_sb(nc, ctx):
    E = ctx.enter_context
    sb = {}
    sb["h1f"] = E(nc.sbuf_tensor("h1f", [128, 32 * 2051], BF16))
    sb["xld"] = E(nc.sbuf_tensor("xld", [128, 4096], F32))
    sb["h1t"] = E(nc.sbuf_tensor("h1t", [128, 4096], BF16))
    sb["wta"] = E(nc.sbuf_tensor("wta", [128, 8192], BF16))
    sb["xcb"] = E(nc.sbuf_tensor("xcb", [128, 2 * 2051], BF16))
    sb["cacc"] = E(nc.sbuf_tensor("cacc", [128, 2051], F32))
    sb["cacd"] = E(nc.sbuf_tensor("cacd", [128, 2051], F32))
    sb["xcv"] = E(nc.sbuf_tensor("xcv", [128, 2048], BF16))
    sb["xT"] = E(nc.sbuf_tensor("xT", [128, 2048], BF16))
    sb["cwsb"] = E(nc.sbuf_tensor("cwsb", [128, 320], F32))
    sb["cbsb"] = E(nc.sbuf_tensor("cbsb", [128, 80], F32))
    sb["dtb"] = E(nc.sbuf_tensor("dtb", [128, 1], F32))
    sb["idbf"] = E(nc.sbuf_tensor("idbf", [128, 128], BF16))
    sb["cols"] = E(nc.sbuf_tensor("cols", [128, 16], F32))
    return sb


def alloc_phase1_ps(nc, ctx):
    E = ctx.enter_context
    ps = {}
    ps["pmm"] = [E(nc.psum_tensor(f"pmm{i}", [128, 512], F32)) for i in range(4)]
    ps["ptr"] = [E(nc.psum_tensor(f"ptr{i}", [128, 1024], BF16)) for i in range(2)]
    return ps


def build_phase1(debug=False):
    nc = bass.Bass("TRN2", target_bir_lowering=False)
    io = build_phase1_io(nc)
    if debug:
        io["h1dbg"] = nc.dram_tensor("h1dbg", [128, 32 * 2051], BF16,
                                     kind="ExternalOutput").ap()
    with ExitStack() as ctx:
        sb = alloc_phase1_sb(nc, ctx)
        ps = alloc_phase1_ps(nc, ctx)
        p = Prog()
        p.sem("qs", "qg", "pe", "dve", "act", "pool", "bar")
        emit_phase1(p, io, sb, ps)
        sems = {n: ctx.enter_context(nc.semaphore(n)) for n in p.cnt}
        block = ctx.enter_context(nc.Block())
        p.install(block, sems)
    return nc



def emit_phase2(p, io, sb, ps):
    xtm_sp, bc_sp, dt_sp = io["xtm_sp"], io["bc_sp"], io["dt_sp"]
    ypart_sp, slo_sp = io["ypart_sp"], io["slo_sp"]
    e3_sp, expcs_sp = io["e3_sp"], io["expcs_sp"]
    csfl, cdrow_d, cumrow_d = io["csfl"], io["cdrow_d"], io["cumrow_d"]

    xtc = sb["xtc"]; xdt = sb["xdt"]; xdtd = sb["xdtd"]; xD = sb["xD"]
    carry = sb["carry"]; carry_bf = sb["carry_bf"]; ybf = sb["ybf"]
    bcsl = sb["bcsl"]; e_sb = sb["e_sb"]; sc_sb = sb["sc_sb"]
    t1s = sb["t1s"]; t2s = sb["t2s"]
    bfm = sb["bfm"]; cfm = sb["cfm"]; btm = sb["btm"]
    chk = sb["chk"]
    dsb = sb["dsb"]; trl = sb["trl"]; idb = sb["idb"]; idf = sb["idf"]
    acol = sb["acol"]; cbm = sb["cbm"]; zz = sb["zz"]
    cdx = sb["cdx"]; cdm = sb["cdm"]; e3s = sb["e3s"]; crow = sb["crow"]
    dtT = sb["dtT"]; decT = sb["decT"]

    pY, pX = ps["pY"], ps["pX"]
    pcb, ptf, ptb = ps["pcb"], ps["ptf"], ps["ptb"]

    CROW = rowstep(chk)
    dtc = chk[:, 0:128]
    dA = chk[:, 128:256]
    cs = chk[:, 256:384]
    csT = chk[:, 384:512]
    expT = chk[:, 512:640]
    dec = chk[:, 640:768]
    ccol = chk[:, 768:769]
    cdc = chk[:, 769:770]

    p.dma("sync", dsb[:, :], apv(io["drow_d"], 0, [[0, 128], [1, 128]]), "qs")
    p.dma("sync", trl[:, :], io["tril_d"], "qs")
    p.dma("sync", idb[:, :], io["idbf_d"], "qs")
    p.dma("sync", idf[:, :], io["idf32_d"], "qs")
    qs_set = p.dma("sync", acol[:, :], io["acol_d"], "qs")
    bar0 = p.quiesce()
    p.on("vector", lambda e: e.memset(carry[:, :], 0.0))
    p.on("vector", lambda e: e.memset(carry_bf[:, :], 0.0))
    p.on("vector", lambda e: e.memset(zz[:, :], 0.0))
    d_init = p.on("vector", lambda e: e.memset(chk[:, 768:770], 0.0),
                  inc=[("dve", 1)])["dve"]
    setup_dve = d_init

    gs = {}
    chk_free = []
    xfree = {}
    prev_spill = None
    cbf_prev = {}
    st_x = p.stream("p2x", "sync", 1)
    st_b = p.stream("p2b", "sync", 1)
    st_c = p.stream("p2c", "sync", 1)
    st_d = p.stream("p2d", "sync", 1)
    st_cs = p.stream("p2s", "sync", 1)
    st_bl = p.stream("p2l", "sync", 2)
    st_r = p.stream("p2r", "sync", 1)
    st_e = p.stream("p2e", "sync", 1)
    st_u = p.stream("p2u", "sync", 1)
    st_m = p.stream("p2m", "sync", 1)
    st_f = p.stream("p2f", "sync", 1)
    st_y = p.stream("p2y", "sync", 1)

    for c in range(NCH):
        qx = st_x.load(xtc[:, :], xtm_sp[c * 128:(c + 1) * 128, :],
                       wait=xfree.get("x", []) + [("bar", bar0)])
        qb = st_b.load(apv(bfm, 0, [[1024, 128], [128, 8], [1, 128]]),
                       apv(bc_sp, c * 128, [[T, 128], [128 * T, 8], [1, 128]]),
                       wait=xfree.get("bc", []))
        qc = st_c.load(apv(cfm, 0, [[1024, 128], [128, 8], [1, 128]]),
                       apv(bc_sp, 1024 * T + c * 128,
                           [[T, 128], [128 * T, 8], [1, 128]]))
        qd = st_d.load(dtc, dt_sp[:, c * 128:(c + 1) * 128],
                       wait=xfree.get("dt", []))

        d1 = p.on("vector", lambda e: e.tensor_scalar(
            dA, dtc, acol[:, 0:1], None, ALU.mult),
            wait=[qd, ("dve", d_init)] + chk_free,
            inc=[("dve", 1)])["dve"]
        d2 = p.on("vector", lambda e: e.tensor_tensor_scan(
            cs, dA, zz[:, 0:128], 0.0, ALU.add, ALU.add),
            wait=[("dve", d1)], inc=[("dve", 1)])["dve"]
        petr1 = p.on("tensor", lambda e: e.transpose(ptf[:, 0:128], cs, idf[:, :]),
                     wait=[("dve", d2)], inc=[("pe", 1)])["pe"]

        def ev_csT(e):
            e.copy(csT, ptf[:, 0:128])
            return e.activation(expT, ptf[:, 0:128], AF.Exp)
        a_csT = p.on("scalar", ev_csT, wait=[("pe", petr1)], inc=[("act", 1)])["act"]
        a_dec = p.on("scalar", lambda e: e.activation(
            dec, cs, AF.Exp, scale=-1.0, bias=cs[:, 127:128]),
            wait=[("dve", d2)], inc=[("act", 1)])["act"]
        a_cd = p.on("scalar", lambda e: e.activation(
            cdc, cs[:, 127:128], AF.Exp), wait=[("act", a_dec)],
            inc=[("act", 1)])["act"]
        petr2 = p.on("tensor", lambda e: e.transpose(ptf[:, 128:256], dtc, idf[:, :]),
                     wait=[qd, ("act", a_csT)], inc=[("pe", 1)])["pe"]
        a_dtT = p.on("scalar", lambda e: e.copy(dtT[:, :], ptf[:, 128:256]),
                     wait=[("pe", petr2)] + xfree.get("dtT", []),
                     inc=[("act", 1)])["act"]
        petr3 = p.on("tensor", lambda e: e.transpose(ptf[:, 256:384], dec, idf[:, :]),
                     wait=[("act", a_dtT)], inc=[("pe", 1)])["pe"]
        a_decT = p.on("scalar", lambda e: e.copy(decT[:, :], ptf[:, 256:384]),
                      wait=[("pe", petr3)] + xfree.get("decT", []),
                      inc=[("act", 1)])["act"]
        petr4 = p.on("tensor", lambda e: e.transpose(
            ptf[0:1, 384:512], cdc, idf[:, :]),
            wait=[("act", a_decT)], inc=[("pe", 1)])["pe"]
        a_crow = p.on("scalar", lambda e: e.copy(
            crow[0:1, 0:128], apv(ptf, 384, [[512, 1], [1, 128]])),
            wait=[("pe", petr4)] + xfree.get("crow", []),
            inc=[("act", 1)])["act"]

        q_csf = st_cs.load(apv(csfl, c * 16384, [[128, 128], [1, 128]]),
                           cs, wait=[("act", a_csT)] + xfree.get("csf", []))
        q_cdr = st_r.load(cdrow_d[c:c + 1, :], crow[0:1, 0:128],
                          wait=[("act", a_crow)])
        q_cdx = st_e.load(cdx[:, :],
                          apv(cdrow_d, c * 128, [[0, 128], [1, 128]]),
                          wait=[q_cdr] + xfree.get("cdx", []))
        petr5 = p.on("tensor", lambda e: e.transpose(
            ptf[0:1, 384:512], ccol, idf[:, :]),
            wait=[("act", a_crow), ("dve", d_init)], inc=[("pe", 1)])["pe"]
        a_cum = p.on("scalar", lambda e: e.activation(
            crow[0:1, 128:256], apv(ptf, 384, [[512, 1], [1, 128]]), AF.Exp),
            wait=[("pe", petr5)] + xfree.get("crow2", []),
            inc=[("act", 1)])["act"]
        q_cur = st_u.load(cumrow_d[c:c + 1, :], crow[0:1, 128:256],
                          wait=[("act", a_cum)])
        q_cum = st_m.load(cdm[:, :],
                          apv(cumrow_d, c * 128, [[0, 128], [1, 128]]),
                          wait=[q_cur] + xfree.get("cdm", []))
        d_cul = p.on("vector", lambda e: e.tensor_tensor(
            ccol, ccol, cs[:, 127:128], ALU.add),
            wait=[("pe", petr5)], inc=[("dve", 1)])["dve"]
        d_init = d_cul

        po_e3 = p.on("gpsimd", lambda e: e.tensor_tensor(
            e3s[:, :], expT, cdm[:, :], ALU.mult),
            wait=[q_cum, ("act", a_csT)] + xfree.get("e3s", []),
            inc=[("pool", 1)])["pool"]
        q_e3 = st_f.load(e3_sp[c], e3s[:, :], wait=[("pool", po_e3)])
        q_ex = p.dma("sync", expcs_sp[c], expT, "qs", wait=[("act", a_csT)])

        po_xdt = p.on("gpsimd", lambda e: e.tensor_tensor(
            xdt[:, :], xtc[:, :],
            apv(dtT, 0, [[128, 128], [1, 128], [0, 64]]), ALU.mult),
            wait=[qx, ("act", a_dtT)] + xfree.get("xdt", []),
            inc=[("pool", 1)])["pool"]
        po_xdtd = p.on("gpsimd", lambda e: e.tensor_tensor(
            xdtd[:, :], xdt[:, :],
            apv(decT, 0, [[128, 128], [1, 128], [0, 64]]), ALU.mult),
            wait=[("act", a_decT)] + xfree.get("xdtd", []),
            inc=[("pool", 1)])["pool"]
        po_xD = p.on("gpsimd", lambda e: e.tensor_tensor(
            xD[:, :], xtc[:, :],
            apv(dsb, 0, [[128, 128], [1, 128], [0, 64]]), ALU.mult),
            wait=xfree.get("xD", []), inc=[("pool", 1)])["pool"]

        for g in range(8):
            gb = g % 2
            w_bc = [("act", gs[("aE", g - 2)])] if ("aE", g - 2) in gs else []
            q_bc = st_bl.load(bcsl[:, gb * 2048:(gb + 1) * 2048],
                              apv(csfl, c * 16384 + g * 2048,
                                  [[0, 128], [1, 2048]]),
                              wait=[q_csf] + w_bc)
            w_cb = [("dve", gs[("cbm", g - 1)])] if ("cbm", g - 1) in gs else []
            pe_cb = p.on("tensor", lambda e, g=g: e.matmul(
                pcb[:, :], bfm[:, g * 128:(g + 1) * 128],
                cfm[:, g * 128:(g + 1) * 128], start=True, stop=True),
                wait=[qb, qc] + w_cb, inc=[("pe", 1)])["pe"]
            d_cbm = p.on("vector", lambda e: e.tensor_tensor(
                cbm[:, :], pcb[:, :], trl[:, :], ALU.mult),
                wait=[("pe", pe_cb)], inc=[("dve", 1)])["dve"]
            gs[("cbm", g)] = d_cbm

            po_d = p.on("gpsimd", lambda e, g=g, gb=gb: e.tensor_tensor(
                bcsl[:, gb * 2048:(gb + 1) * 2048],
                bcsl[:, gb * 2048:(gb + 1) * 2048],
                apv(chk, 384 + g * 16, [[CROW, 128], [1, 16], [0, 128]]),
                ALU.subtract), wait=[q_bc, ("act", a_csT)],
                inc=[("pool", 1)])["pool"]
            po_m = p.on("gpsimd", lambda e, gb=gb: e.tensor_scalar(
                bcsl[:, gb * 2048:(gb + 1) * 2048],
                bcsl[:, gb * 2048:(gb + 1) * 2048], 0.0, None, ALU.min),
                wait=[("pool", po_d)], inc=[("pool", 1)])["pool"]
            gs[("pod", g)] = po_m
            w_esb = [("dve", gs[("sc", g - 2)])] if ("sc", g - 2) in gs else []
            a_E = p.on("scalar", lambda e, gb=gb: e.activation(
                e_sb[:, gb * 2048:(gb + 1) * 2048],
                bcsl[:, gb * 2048:(gb + 1) * 2048], AF.Exp),
                wait=[("pool", po_m)] + w_esb, inc=[("act", 1)])["act"]
            gs[("aE", g)] = a_E
            w_sc = [("pe", gs[("ydiag", g - 2)])] if ("ydiag", g - 2) in gs else []
            d_sc = p.on("vector", lambda e, g=g, gb=gb: e.tensor_tensor(
                sc_sb[:, gb * 2048:(gb + 1) * 2048],
                e_sb[:, gb * 2048:(gb + 1) * 2048],
                apv(cbm, 0, [[128, 128], [0, 16], [1, 128]]), ALU.mult),
                wait=[("act", a_E), ("dve", d_cbm)] + w_sc,
                inc=[("dve", 1)])["dve"]
            gs[("sc", g)] = d_sc

            w_btm = [("pe", gs[("states", g - 1)])] \
                if ("states", g - 1) in gs else []
            pe_bt = p.on("tensor", lambda e, g=g: e.transpose(
                ptb[:, :], bfm[:, g * 128:(g + 1) * 128], idb[:, :]),
                wait=[qb] + w_btm +
                ([("act", gs[("btm_ev", g - 1)])] if ("btm_ev", g - 1) in gs else []),
                inc=[("pe", 1)])["pe"]
            a_bt = p.on("scalar", lambda e: e.copy(btm[:, :], ptb[:, :]),
                        wait=[("pe", pe_bt)], inc=[("act", 1)])["act"]
            gs[("btm_ev", g)] = a_bt

            w_y = [("dve", gs[("ybf", g - 1)])] if ("ybf", g - 1) in gs else []

            def f_ydiag(e, g=g, gb=gb):
                last = None
                for h in range(16):
                    last = e.matmul(pY[:, h * 64:(h + 1) * 64],
                                    sc_sb[:, gb * 2048 + h * 128:
                                          gb * 2048 + (h + 1) * 128],
                                    xdt[:, (g * 16 + h) * 64:(g * 16 + h + 1) * 64],
                                    start=(h % 8 == 0), stop=False)
                for k in range(2):
                    last = e.matmul(pY[:, k * 512:(k + 1) * 512], idb[:, :],
                                    xD[:, g * 1024 + k * 512: g * 1024 + (k + 1) * 512],
                                    start=False, stop=True)
                return last
            pe_yd = p.on("tensor", f_ydiag,
                         wait=[("dve", d_sc), ("pool", po_xD)] + w_y,
                         inc=[("pe", 1)])["pe"]
            gs[("ydiag", g)] = pe_yd

            cb_w = [("act", cbf_prev[g])] if g in cbf_prev \
                else [("dve", setup_dve)]
            w_px = [("dve", gs[("carry", g - 1)])] if ("carry", g - 1) in gs else []

            def f_yoff(e, g=g):
                last = None
                for k in range(2):
                    last = e.matmul(pX[:, k * 512:(k + 1) * 512],
                                    cfm[:, g * 128:(g + 1) * 128],
                                    carry_bf[:, g * 1024 + k * 512:
                                             g * 1024 + (k + 1) * 512],
                                    start=True, stop=True)
                return last
            pe_yo = p.on("tensor", f_yoff,
                         wait=[qc] + w_px + cb_w, inc=[("pe", 1)])["pe"]

            d_t1 = p.on("vector", lambda e, g=g: e.tensor_tensor(
                t1s[:, :], pX[:, :],
                apv(chk, 512 + g * 16, [[CROW, 128], [1, 16], [0, 64]]), ALU.mult),
                wait=[("pe", pe_yo), ("act", a_csT)], inc=[("dve", 1)])["dve"]
            gs[("t1", g)] = d_t1

            def f_st(e, g=g):
                last = None
                for k in range(2):
                    last = e.matmul(pX[:, k * 512:(k + 1) * 512], btm[:, :],
                                    xdtd[:, g * 1024 + k * 512:
                                         g * 1024 + (k + 1) * 512],
                                    start=True, stop=True)
                return last
            pe_st = p.on("tensor", f_st,
                         wait=[("act", a_bt), ("pool", po_xdtd), ("dve", d_t1)],
                         inc=[("pe", 1)])["pe"]
            gs[("states", g)] = pe_st
            d_ybf = p.on("vector", lambda e, g=g: e.tensor_tensor(
                ybf[:, g * 1024:(g + 1) * 1024], t1s[:, :], pY[:, :], ALU.add),
                wait=[("pe", pe_yd), ("dve", d_t1)] +
                ([prev_spill] if prev_spill is not None and g == 0 else []),
                inc=[("dve", 1)])["dve"]
            gs[("ybf", g)] = d_ybf

            w_t2 = [("dve", gs[("carry", g - 1)])] if ("carry", g - 1) in gs else []
            po_t2 = p.on("gpsimd", lambda e, g=g: e.tensor_tensor(
                t2s[:, :], carry[:, g * 1024:(g + 1) * 1024],
                apv(cdx, g * 16, [[128, 128], [1, 16], [0, 64]]), ALU.mult),
                wait=[q_cdx, ("pe", pe_yo)] + w_t2,
                inc=[("pool", 1)])["pool"]
            d_cy = p.on("vector", lambda e, g=g: e.tensor_tensor(
                carry[:, g * 1024:(g + 1) * 1024], t2s[:, :], pX[:, :], ALU.add),
                wait=[("pool", po_t2), ("pe", pe_st)], inc=[("dve", 1)])["dve"]
            gs[("carry", g)] = d_cy
            a_cbf = p.on("scalar", lambda e, g=g: e.copy(
                carry_bf[:, g * 1024:(g + 1) * 1024],
                carry[:, g * 1024:(g + 1) * 1024]),
                wait=[("dve", d_cy)], inc=[("act", 1)])["act"]
            cbf_prev[g] = a_cbf

        prev_spill = st_y.load(ypart_sp[c * 128:(c + 1) * 128, :], ybf[:, :],
                               wait=[("dve", gs[("ybf", 7)])])

        chk_free = [("act", a_cum), ("dve", max(gs[("t1", 7)], d_cul)),
                    ("pool", max(gs[("pod", 7)], po_e3)),
                    ("qs", q_ex), q_csf, ("pe", petr5)]
        xfree = {
            "x": [("pool", po_xD)],
            "bc": [("pe", gs[("states", 7)])],
            "dt": [("dve", d1), ("pe", petr2)],
            "dtT": [("pool", po_xdt)],
            "decT": [("pool", po_xdtd)],
            "crow": [q_cdr],
            "crow2": [q_cur],
            "cdx": [("pool", po_t2)],
            "cdm": [("pool", po_e3)],
            "e3s": [q_e3],
            "csf": [("pool", gs[("pod", 7)])],
            "xdt": [("pe", gs[("ydiag", 7)])],
            "xdtd": [("pe", gs[("states", 7)])],
            "xD": [("pe", gs[("ydiag", 7)])],
        }
        gs = {}

    q_s = p.dma("sync", slo_sp[:, :], carry_bf[:, :], "qs",
                wait=[("act", cbf_prev[7])])
    p.on("sync", None, wait=[(s, p.v(s)) for s in p.dma_sems if p.v(s) > 0])


def build_phase2_io(nc):
    io = {}
    io["xtm_sp"] = nc.dram_tensor("xtm_sp", [T, INTER], BF16, kind="ExternalInput").ap()
    io["bc_sp"] = nc.dram_tensor("bc_sp", [2048, T], BF16, kind="ExternalInput").ap()
    io["dt_sp"] = nc.dram_tensor("dt_sp", [128, T], F32, kind="ExternalInput").ap()
    io["drow_d"] = nc.dram_tensor("drow_d", [1, 128], F32, kind="ExternalInput").ap()
    io["tril_d"] = nc.dram_tensor("tril_d", [128, 128], F32, kind="ExternalInput").ap()
    io["idbf_d"] = nc.dram_tensor("idbf_d", [128, 128], BF16, kind="ExternalInput").ap()
    io["idf32_d"] = nc.dram_tensor("idf32_d", [128, 128], F32, kind="ExternalInput").ap()
    io["acol_d"] = nc.dram_tensor("acol_d", [128, 1], F32, kind="ExternalInput").ap()
    io["ypart_sp"] = nc.dram_tensor("ypart_sp", [T, INTER], BF16,
                                    kind="ExternalOutput").ap()
    io["slo_sp"] = nc.dram_tensor("slo_sp", [128, INTER], BF16,
                                  kind="ExternalOutput").ap()
    io["e3_sp"] = nc.dram_tensor("e3_sp", [NCH, 128, 128], F32,
                                 kind="ExternalOutput").ap()
    io["expcs_sp"] = nc.dram_tensor("expcs_sp", [NCH, 128, 128], F32,
                                    kind="ExternalOutput").ap()
    io["csfl"] = nc.dram_tensor("csfl", [NCH, 16384], F32, kind="Internal").ap()
    io["cdrow_d"] = nc.dram_tensor("cdrow_d", [NCH, 128], F32, kind="Internal").ap()
    io["cumrow_d"] = nc.dram_tensor("cumrow_d", [NCH, 128], F32, kind="Internal").ap()
    return io


def alloc_phase2_sb(nc, ctx):
    E = ctx.enter_context
    sb = {}
    sb["xtc"] = E(nc.sbuf_tensor("xtc", [128, INTER], BF16))
    sb["xdt"] = E(nc.sbuf_tensor("xdt", [128, INTER], BF16))
    sb["xdtd"] = E(nc.sbuf_tensor("xdtd", [128, INTER], BF16))
    sb["xD"] = E(nc.sbuf_tensor("xD", [128, INTER], BF16))
    sb["carry"] = E(nc.sbuf_tensor("carry", [128, INTER], F32))
    sb["carry_bf"] = E(nc.sbuf_tensor("carry_bf", [128, INTER], BF16))
    sb["ybf"] = E(nc.sbuf_tensor("ybf", [128, INTER], BF16))
    sb["bcsl"] = E(nc.sbuf_tensor("bcsl", [128, 4096], F32))
    sb["e_sb"] = E(nc.sbuf_tensor("e_sb", [128, 4096], BF16))
    sb["sc_sb"] = E(nc.sbuf_tensor("sc_sb", [128, 4096], BF16))
    sb["t1s"] = E(nc.sbuf_tensor("t1s", [128, 1024], F32))
    sb["t2s"] = E(nc.sbuf_tensor("t2s", [128, 1024], F32))
    sb["bfm"] = E(nc.sbuf_tensor("bfm", [128, 1024], BF16))
    sb["cfm"] = E(nc.sbuf_tensor("cfm", [128, 1024], BF16))
    sb["btm"] = E(nc.sbuf_tensor("btm", [128, 128], BF16))
    sb["chk"] = E(nc.sbuf_tensor("chk", [128, 1024], F32))
    sb["dtT"] = E(nc.sbuf_tensor("dtT", [128, 128], BF16))
    sb["decT"] = E(nc.sbuf_tensor("decT", [128, 128], BF16))
    sb["dsb"] = E(nc.sbuf_tensor("dsb", [128, 128], F32))
    sb["trl"] = E(nc.sbuf_tensor("trl", [128, 128], F32))
    sb["idb"] = E(nc.sbuf_tensor("idb", [128, 128], BF16))
    sb["idf"] = E(nc.sbuf_tensor("idf", [128, 128], F32))
    sb["acol"] = E(nc.sbuf_tensor("acol", [128, 1], F32))
    sb["cbm"] = E(nc.sbuf_tensor("cbm", [128, 128], F32))
    sb["zz"] = E(nc.sbuf_tensor("zz", [128, 128], F32))
    sb["cdx"] = E(nc.sbuf_tensor("cdx", [128, 128], F32))
    sb["cdm"] = E(nc.sbuf_tensor("cdm", [128, 128], F32))
    sb["e3s"] = E(nc.sbuf_tensor("e3s", [128, 128], F32))
    sb["crow"] = E(nc.sbuf_tensor("crow", [1, 256], F32))
    return sb


def alloc_phase2_ps(nc, ctx):
    E = ctx.enter_context
    ps = {}
    ps["pY"] = E(nc.psum_tensor("pY", [128, 1024], F32))
    ps["pX"] = E(nc.psum_tensor("pX", [128, 1024], F32))
    ps["pcb"] = E(nc.psum_tensor("pcb", [128, 128], F32))
    ps["ptf"] = E(nc.psum_tensor("ptf", [128, 512], F32))
    ps["ptb"] = E(nc.psum_tensor("ptb", [128, 128], BF16))
    return ps


def build_phase2():
    nc = bass.Bass("TRN2", target_bir_lowering=False)
    io = build_phase2_io(nc)
    with ExitStack() as ctx:
        sb = alloc_phase2_sb(nc, ctx)
        ps = alloc_phase2_ps(nc, ctx)
        p = Prog()
        p.sem("qs", "qg", "pe", "dve", "act", "pool", "bar")
        emit_phase2(p, io, sb, ps)
        sems = {n: ctx.enter_context(nc.semaphore(n)) for n in p.cnt}
        block = ctx.enter_context(nc.Block())
        p.install(block, sems)
    return nc


def host_layout_phase2(inputs, p1outs):
    D = np.asarray(inputs["L0_attn_D"], np.float32)
    drow = D.reshape(1, 128)
    tril = np.tril(np.ones((128, 128), np.float32))
    tril = np.triu(np.ones((128, 128), np.float32))
    idbf = np.eye(128, dtype=np.float32).astype(NPBF)
    idf32 = np.eye(128, dtype=np.float32)
    acol = -np.arange(1, 129, dtype=np.float32).reshape(128, 1)
    maps = []
    for core in range(NCORES):
        maps.append({"xtm_sp": p1outs[core]["xtm_sp"],
                     "bc_sp": p1outs[core]["bc_sp"],
                     "dt_sp": p1outs[core]["dt_sp"],
                     "drow_d": drow, "tril_d": tril, "idbf_d": idbf,
                     "idf32_d": idf32, "acol_d": acol})
    return maps



def emit_prestage(p, io, sb):
    wpre = sb["wpre"]
    nwsb = sb["nwsb"]
    q_nw = p.dma("sync", nwsb[:, :], io["normw_d"], "qs")
    st_ld = p.stream("pgl", "gpsimd", 2)
    st_st = p.stream("pst", "sync", 2)
    st_prev = {}
    for i in range(64):
        b = i % 2
        wv = wpre[:, b * 4096:(b + 1) * 4096]
        ww = [st_prev[i - 2]] if i - 2 in st_prev else []
        qi = st_ld.load(wv, io["WoutT"][i * 128:(i + 1) * 128, :], wait=ww)
        dv = p.on("vector", lambda e, wv=wv, i=i: e.tensor_scalar(
            wv, wv, nwsb[:, i:i + 1], None, ALU.mult),
            wait=[qi, ("qs", q_nw)], inc=[("dve", 1)])["dve"]
        st_prev[i] = st_st.load(io["wout_bf"][i * 128:(i + 1) * 128, :], wv,
                                wait=[("dve", dv)])
    jobs = []
    for cch in range(0, FH, 4096):
        w = min(4096, FH - cch)
        for r in range(8):
            jobs.append((io["w1qT"], io["w1_bf"], r * 128, cch, 128, w))
            jobs.append((io["w3qT"], io["w3_bf"], r * 128, cch, 128, w))
        jobs.append((io["b1T"], io["b1_bf"], 0, cch, 2, w))
        jobs.append((io["b3T"], io["b3_bf"], 0, cch, 2, w))
    for r in range(22):
        jobs.append((io["w2qT"], io["w2_bf"], r * 128, 0, 128, 4096))
    for r in range(32):
        jobs.append((io["a1T"], io["a1_bf"], r * 128, 0, 128, 2))
        jobs.append((io["a3T"], io["a3_bf"], r * 128, 0, 128, 2))
    for r in range(88):
        jobs.append((io["a2T"], io["a2_bf"], r * 128, 0, 128, 2))
    jobs.append((io["b2T"], io["b2_bf"], 0, 0, 2, 4096))
    ji = 64
    for (srcd, dstd, r0, c0, nr, ncol) in jobs:
        b = ji % 2
        wv = wpre[0:nr, b * 4096:b * 4096 + ncol]
        ww = [st_prev[ji - 2]] if ji - 2 in st_prev else []
        qi = st_ld.load(wv, srcd[r0:r0 + nr, c0:c0 + ncol], wait=ww)
        st_prev[ji] = st_st.load(dstd[r0:r0 + nr, c0:c0 + ncol], wv, wait=[qi])
        ji += 1


def emit_phase25(p, io, sb, ps):
    smv = sb["smv"]; ccf = sb["ccf"]; e3t = sb["e3t"]
    ytl = sb["ytl"]; ufc = sb["ufc"]; mcol = sb["mcol"]
    pQ = ps["pQ"]

    barp = p.quiesce()
    q_m = p.dma("sync", mcol[:, :], io["mask_d"], "qs", wait=[("bar", barp)])
    q_sin = p.dma("sync", smv[:, :], io["sin_d"], "qs")
    d_sm = p.on("vector", lambda e: e.tensor_scalar(
        smv[:, :], smv[:, :], mcol[:, 0:1], None, ALU.mult),
        wait=[("qs", q_sin)], inc=[("dve", 1)])["dve"]

    prev = {}
    pq_free = {0: None, 1: None}
    st_yl = p.stream("cyl", "sync", 2)
    st_ys = p.stream("cys", "sync", 2)
    st_cc = p.stream("ccc", "sync", 1)
    st_ce = p.stream("cce", "sync", 2)
    for c in range(NCH):
        yb = c % 2
        wy = [prev[("yst", c - 2)]] if ("yst", c - 2) in prev \
            else [("bar", barp)]
        q_y = st_yl.load(ytl[:, yb * 8192:(yb + 1) * 8192],
                         io["ypart_sp"][c * 128:(c + 1) * 128, :], wait=wy)
        wc = [("pe", prev[("mm", c - 1)])] if ("mm", c - 1) in prev else []
        q_c = st_cc.load(apv(ccf, 0, [[1024, 128], [128, 8], [1, 128]]),
                         apv(io["bc_sp"], 1024 * T + c * 128,
                             [[T, 128], [128 * T, 8], [1, 128]]), wait=wc)
        we = [("dve", prev[("e3", c - 2)])] if ("e3", c - 2) in prev else []
        q_e = st_ce.load(e3t[:, yb * 128:(yb + 1) * 128],
                         io["e3_sp"][c], wait=we)
        for g in range(8):
            pb = g % 2
            wp = [("dve", pq_free[pb])] if pq_free[pb] is not None else []

            def f_mm(e, g=g, pb=pb):
                last = None
                for k in range(2):
                    last = e.matmul(pQ[pb][:, k * 512:(k + 1) * 512],
                                    ccf[:, g * 128:(g + 1) * 128],
                                    smv[:, g * 1024 + k * 512:
                                        g * 1024 + (k + 1) * 512],
                                    start=True, stop=True)
                return last
            mm = p.on("tensor", f_mm,
                      wait=[q_c, ("dve", d_sm)] + wp,
                      inc=[("pe", 1)])["pe"]
            prev[("mm", c)] = mm
            wu = [("pool", prev[("y", c, g - 1)])] if ("y", c, g - 1) in prev \
                else ([("pool", prev[("ylast", c - 1)])]
                      if ("ylast", c - 1) in prev else [])
            dd = p.on("vector", lambda e, g=g, pb=pb, yb=yb: e.tensor_tensor(
                ufc[:, :], pQ[pb][:, :],
                apv(e3t, yb * 128 + g * 16, [[256, 128], [1, 16], [0, 64]]),
                ALU.mult),
                wait=[("pe", mm), q_e] + wu,
                inc=[("dve", 1)])["dve"]
            pq_free[pb] = dd
            prev[("e3", c)] = dd
            po = p.on("gpsimd", lambda e, g=g, yb=yb: e.tensor_tensor(
                ytl[:, yb * 8192 + g * 1024: yb * 8192 + (g + 1) * 1024],
                ytl[:, yb * 8192 + g * 1024: yb * 8192 + (g + 1) * 1024],
                ufc[:, :], ALU.add),
                wait=[("dve", dd), q_y], inc=[("pool", 1)])["pool"]
            prev[("y", c, g)] = po
        prev[("ylast", c)] = prev[("y", c, 7)]
        prev[("yst", c)] = st_ys.load(io["ycorr"][c * 128:(c + 1) * 128, :],
                                      ytl[:, yb * 8192:(yb + 1) * 8192],
                                      wait=[("pool", prev[("y", c, 7)])])
    return prev[("yst", NCH - 1)]


def emit_rowln(p, src, cols, sink, wait_pre, c0=16):
    c_sum, c_mean, c_sqa, c_sqb, c_var, c_m2 = (
        cols[:, c0 + i:c0 + i + 1] for i in range(6))
    c_rstd = cols[:, c0 + 4:c0 + 5]

    def f_sq(e):
        e.activation(sink[:, 0:2048], src[:, 0:2048], AF.Square, accum_out=c_sqa)
        return e.activation(sink[:, 2048:4096], src[:, 2048:4096], AF.Square,
                            accum_out=c_sqb)
    a_sq = p.on("scalar", f_sq, wait=wait_pre, inc=[("act", 1)])["act"]
    da = p.on("vector", lambda e: e.tensor_reduce(c_sum, src, AX.X, ALU.add),
              wait=wait_pre, inc=[("dve", 1)])["dve"]
    db = p.on("vector", lambda e: e.tensor_scalar_mul(c_mean, c_sum, 1.0 / DM),
              wait=[("dve", da)], inc=[("dve", 1)])["dve"]
    dc = p.on("vector", lambda e: e.tensor_tensor(c_m2, c_mean, c_mean, ALU.mult),
              wait=[("dve", db)], inc=[("dve", 1)])["dve"]
    dd = p.on("vector", lambda e: e.tensor_tensor(c_var, c_sqa, c_sqb, ALU.add),
              wait=[("act", a_sq), ("dve", dc)], inc=[("dve", 1)])["dve"]
    de = p.on("vector", lambda e: e.tensor_scalar(
        c_var, c_var, 1.0 / DM, EPS, ALU.mult, ALU.add),
        wait=[("dve", dd)], inc=[("dve", 1)])["dve"]
    df = p.on("vector", lambda e: e.tensor_tensor(c_var, c_var, c_m2, ALU.subtract),
              wait=[("dve", de)], inc=[("dve", 1)])["dve"]
    a_l = p.on("scalar", lambda e: e.activation(c_rstd, c_var, AF.Ln),
               wait=[("dve", df)], inc=[("act", 1)])["act"]
    a_r = p.on("scalar", lambda e: e.activation(c_rstd, c_rstd, AF.Exp, scale=-0.5),
               wait=[("act", a_l)], inc=[("act", 1)])["act"]
    return (c_mean, c_rstd, a_r, df)


PARANOID = False


def emit_phase3(p, io, sb, ps, q_corr, debug=False):
    def maybe_bar():
        if PARANOID:
            p.barrier_all()
    S1, S2, S3 = sb["S1"], sb["S2"], sb["S3"]
    xzf, c1f = sb["xzf"], sb["c1f"]
    wob = sb["wob"]; wf1 = sb["wf1"]; wf3 = sb["wf3"]
    b2s, b13 = sb["b2s"], sb["b13"]
    r1b = sb["r1b"]
    a1g, a3g, a2g = sb["a1g"], sb["a3g"], sb["a2g"]
    cols = sb["cols3"]; idb = sb["idb3"]
    pA, pB, pD, pC, pT = ps["pA"], ps["pB"], ps["pD"], ps["pC"], ps["pT"]
    S1R = rowstep(S1)
    po_tt = [pA[:, 0:512], pA[:, 512:1024], pB[:, 0:512], pB[:, 512:1024]]

    bar3a = p.quiesce()
    q_b = p.dma("sync", b2s[:, :], io["b2_bf"], "qs", wait=[("bar", bar3a)])
    p.dma("sync", apv(a1g, 0, [[64, 128], [2, 32], [1, 2]]),
          apv(io["a1_bf"], 0, [[2, 128], [256, 32], [1, 2]]), "qs")
    p.dma("sync", apv(a3g, 0, [[64, 128], [2, 32], [1, 2]]),
          apv(io["a3_bf"], 0, [[2, 128], [256, 32], [1, 2]]), "qs")
    p.dma("sync", apv(a2g, 0, [[176, 128], [2, 88], [1, 2]]),
          apv(io["a2_bf"], 0, [[2, 128], [256, 88], [1, 2]]), "qs")
    q_id = p.dma("sync", idb[:, :], io["idbf_d"], "qs")
    bar3 = p.quiesce()

    st_y3 = p.stream("y3l", "sync", 2)
    st_g3 = p.stream("g3l", "sync", 2)
    st_wo = p.stream("wo", "sync", 2)
    st_w1 = p.stream("wf1l", "sync", 2)
    st_w3 = p.stream("wf3l", "sync", 2)
    st_b1 = p.stream("b1l", "sync", 2)
    st_b3 = p.stream("b3l", "sync", 2)
    st_x3 = p.stream("x3l", "sync", 1)
    st_o3 = p.stream("o3s", "sync", 2)

    fr = {}
    for t4 in range(4):
        tau = t4 * 512
        last_ev = None
        for j in range(4):
            c = t4 * 4 + j
            ypt = S2[:, 0:8192]
            gpt = S2[:, 8192:16384]
            sg = S3[:, 0:8192]
            yg = S3[:, 8192:16384]
            q_y = st_y3.load(ypt, io["ycorr"][c * 128:(c + 1) * 128, :],
                             wait=[("bar", bar3)] + fr.get("S2a", []))
            q_g = st_g3.load(gpt, io["gate_sp"][c * 128:(c + 1) * 128, :])
            a_sg = p.on("scalar", lambda e, sg=sg, gpt=gpt: e.activation(
                sg, gpt, AF.Silu), wait=[q_g] + fr.get("S3a", []),
                inc=[("act", 1)])["act"]
            d_yg = p.on("vector", lambda e, yg=yg, ypt=ypt, sg=sg: e.tensor_tensor(
                yg, ypt, sg, ALU.mult),
                wait=[q_y, ("act", a_sg)] + fr.get("yga", []),
                inc=[("dve", 1)])["dve"]
            fr["S2a"] = [("dve", d_yg)]
            fr["S3a"] = [("dve", d_yg)]

            def f_sq(e, yg=yg):
                last = None
                for g in range(8):
                    last = e.activation(c1f[:, 0:1024],
                                        yg[:, g * 1024:(g + 1) * 1024], AF.Square,
                                        accum_out=cols[:, g:g + 1])
                return last
            a_ssq = p.on("scalar", f_sq,
                         wait=[("dve", d_yg)] + fr.get("c1f", []) +
                         fr.get("gncols", []),
                         inc=[("act", 1)])["act"]
            fr["c1f"] = [("act", a_ssq)]
            d_r1 = p.on("vector", lambda e: e.tensor_scalar(
                cols[:, 8:16], cols[:, 0:8], 1.0 / GROUP, EPS, ALU.mult, ALU.add),
                wait=[("act", a_ssq)], inc=[("dve", 1)])["dve"]
            fr["gncols"] = [("dve", d_r1)]
            a_r2 = p.on("scalar", lambda e: e.activation(
                cols[:, 8:16], cols[:, 8:16], AF.Ln),
                wait=[("dve", d_r1)], inc=[("act", 1)])["act"]
            a_r3 = p.on("scalar", lambda e: e.activation(
                cols[:, 8:16], cols[:, 8:16], AF.Exp, scale=-0.5),
                wait=[("act", a_r2)], inc=[("act", 1)])["act"]

            def f_ygn(e, yg=yg):
                last = None
                for g in range(8):
                    last = e.tensor_scalar(yg[:, g * 1024:(g + 1) * 1024],
                                           yg[:, g * 1024:(g + 1) * 1024],
                                           cols[:, 8 + g:9 + g], None, ALU.mult)
                return last
            d_ygn = p.on("vector", f_ygn, wait=[("act", a_r3)],
                         inc=[("dve", 1)])["dve"]
            pe_tr = None
            for bt in range(8):
                def f_tr(e, bt=bt, yg=yg):
                    last = None
                    for i in range(8):
                        it = bt * 8 + i
                        last = e.transpose(pT[:, i * 128:(i + 1) * 128],
                                           yg[:, it * 128:(it + 1) * 128],
                                           idb[:, :])
                    return last
                pe_tr = p.on("tensor", f_tr,
                             wait=[("dve", d_ygn), ("qs", q_id)] + fr.get("pT", []),
                             inc=[("pe", 1)])["pe"]

                def f_ev(e, bt=bt, j=j):
                    srcp = apv(pT, 0, [[1024, 128], [128, 8], [1, 128]])
                    dstp = apv(S1, (bt * 8) * 512 + j * 128,
                               [[S1R, 128], [512, 8], [1, 128]])
                    return e.copy(dstp, srcp)
                last_ev = p.on("scalar", f_ev,
                               wait=[("pe", pe_tr)] + fr.get("S1a", []),
                               inc=[("act", 1)])["act"]
                fr["pT"] = [("act", last_ev)]
            fr["yga"] = [("pe", pe_tr)]
        maybe_bar()
        ygn_done = last_ev
        if debug and t4 == 0:
            p.dma("sync", io["dbg_ygn"], S1[:, :], "qs", wait=[("act", ygn_done)])

        evB = {}
        mmB = {}
        for jn in range(8):
            for i in range(64):
                b = (jn * 64 + i) % 2
                u = jn * 64 + i
                wiw = [("pe", mmB[u - 2])] if u - 2 in mmB \
                    else fr.get("wob", [("bar", bar3)])
                q_w = st_wo.load(wob[:, b * 512:(b + 1) * 512],
                                 io["wout_bf"][i * 128:(i + 1) * 128,
                                               jn * 512:(jn + 1) * 512],
                                 wait=wiw)

                def f_mm(e, i=i, b=b):
                    last = None
                    for tt in range(4):
                        last = e.matmul(
                            po_tt[tt],
                            S1[:, i * 512 + tt * 128: i * 512 + tt * 128 + 128],
                            wob[:, b * 512:(b + 1) * 512],
                            start=(i == 0), stop=(i == 63))
                    return last
                wpsum = []
                if i == 0:
                    if jn >= 1:
                        wpsum = [("act", evB[(jn - 1, 3)])]
                    else:
                        wpsum = fr.get("pAB", [])
                mmB[u] = p.on("tensor", f_mm,
                              wait=[q_w, ("act", ygn_done)] + wpsum,
                              inc=[("pe", 1)])["pe"]
            for tt in range(4):
                obf = S2[:, tt * 4096:(tt + 1) * 4096]

                def f_ev(e, tt=tt, jn=jn, obf=obf):
                    return e.copy(obf[:, jn * 512:(jn + 1) * 512], po_tt[tt])
                evB[(jn, tt)] = p.on("scalar", f_ev,
                                     wait=[("pe", mmB[jn * 64 + 63])],
                                     inc=[("act", 1)])["act"]

        if debug and t4 == 0:
            p.dma("sync", io["dbg_obf"], S2[:, :], "qs",
                  wait=[("act", evB[(7, 3)])])
        maybe_bar()
        d_x1 = {}
        q_out_prev = fr.get("qout", None)
        for tt in range(4):
            obf = S2[:, tt * 4096:(tt + 1) * 4096]
            x1 = S1[:, tt * 4096:(tt + 1) * 4096]
            w_ev = [("act", evB[(7, tt)])] + fr.get("c1f", []) + \
                fr.get("lncols", [])
            q_x0 = st_x3.load(
                xzf[:, :],
                io["x0c"][3 + tau + tt * 128: 3 + tau + (tt + 1) * 128, :],
                wait=fr.get("xzf", []))
            ln = emit_rowln(p, obf, cols, c1f, w_ev)
            fr["lncols"] = [("dve", ln[3]), ("act", ln[2])]
            d_c1 = p.on("vector", lambda e, obf=obf, m=ln[0], r=ln[1]:
                        e.tensor_scalar(c1f[:, :], obf, m, r,
                                        ALU.subtract, ALU.mult),
                        wait=[("act", ln[2])], inc=[("dve", 1)])["dve"]
            d_x1[tt] = p.on("vector", lambda e, x1=x1: e.tensor_tensor(
                x1, c1f[:, :], xzf[:, :], ALU.add),
                wait=[("dve", d_c1), q_x0], inc=[("dve", 1)])["dve"]
            fr["xzf"] = [("dve", d_x1[tt])]
            fr["c1f"] = [("dve", d_x1[tt])]

        maybe_bar()
        h2f_off = 16384
        tr_last = None
        pe_tr = None
        for tt in range(4):
            x1 = S1[:, tt * 4096:(tt + 1) * 4096]
            ln = emit_rowln(p, x1, cols, c1f,
                            [("dve", d_x1[tt])] + fr.get("c1f", []) +
                            fr.get("lncols", []))
            fr["lncols"] = [("dve", ln[3]), ("act", ln[2])]
            d_h2 = p.on("vector", lambda e, x1=x1, m=ln[0], r=ln[1]:
                        e.tensor_scalar(S2[:, 0:4096], x1, m, r,
                                        ALU.subtract, ALU.mult),
                        wait=[("act", ln[2])] + fr.get("h2t", []),
                        inc=[("dve", 1)])["dve"]
            fr["c1f"] = [("act", ln[2])]
            for bt in range(4):
                def f_tr(e, bt=bt):
                    last = None
                    for i in range(8):
                        it = bt * 8 + i
                        last = e.transpose(pT[:, i * 128:(i + 1) * 128],
                                           S2[:, it * 128:(it + 1) * 128],
                                           idb[:, :])
                    return last
                pe_tr = p.on("tensor", f_tr,
                             wait=[("dve", d_h2)] + fr.get("pT", []),
                             inc=[("pe", 1)])["pe"]

                def f_ev(e, bt=bt, tt=tt):
                    srcp = apv(pT, 0, [[1024, 128], [128, 8], [1, 128]])
                    dstp = apv(S1, h2f_off + (bt * 8) * 512 + tt * 128,
                               [[S1R, 128], [512, 8], [1, 128]])
                    return e.copy(dstp, srcp)
                tr_last = p.on("scalar", f_ev, wait=[("pe", pe_tr)],
                               inc=[("act", 1)])["act"]
                fr["pT"] = [("act", tr_last)]
            fr["h2t"] = [("pe", pe_tr)]
        if debug and t4 == 0:
            p.dma("sync", io["dbg_x1"], S1[:, :], "qs", wait=[("act", tr_last)])
        d_xs = None
        for it in range(8):
            def hv(q, it=it):
                o = h2f_off + (q * 8 + it) * 512
                return S1[:, o:o + 512]
            p.on("vector", lambda e, hv=hv: e.tensor_tensor(
                c1f[:, 0:512], hv(0), hv(1), ALU.add),
                wait=[("act", tr_last)] + fr.get("S3d", []) + fr.get("c1f", []))
            p.on("vector", lambda e, hv=hv: e.tensor_tensor(
                c1f[:, 512:1024], hv(2), hv(3), ALU.add))
            d_xs = p.on("vector", lambda e, it=it: e.tensor_tensor(
                S3[:, it * 512:(it + 1) * 512], c1f[:, 0:512], c1f[:, 512:1024],
                ALU.add), inc=[("dve", 1)])["dve"]
        fr["c1f"] = [("dve", d_xs)]

        maybe_bar()
        def f_r1(e):
            last = None
            for kt in range(32):
                last = e.matmul(pD[0:2, 0:512], a1g[:, kt * 2:(kt + 1) * 2],
                                S1[:, h2f_off + kt * 512:h2f_off + (kt + 1) * 512],
                                start=(kt == 0), stop=(kt == 31))
            return last
        pe_r1 = p.on("tensor", f_r1,
                     wait=[("act", tr_last), ("qs", q_id)] + fr.get("pD", []),
                     inc=[("pe", 1)])["pe"]

        def f_r3(e):
            last = None
            for kt in range(32):
                last = e.matmul(pD[0:2, 512:1024], a3g[:, kt * 2:(kt + 1) * 2],
                                S1[:, h2f_off + kt * 512:h2f_off + (kt + 1) * 512],
                                start=(kt == 0), stop=(kt == 31))
            return last
        pe_r3 = p.on("tensor", f_r3, inc=[("pe", 1)])["pe"]
        a_rr = p.on("scalar", lambda e: (
            e.activation(r1b[0:2, 0:512], pD[0:2, 0:512], AF.Copy, scale=0.5),
            e.activation(r1b[0:2, 512:1024], pD[0:2, 512:1024], AF.Copy,
                         scale=0.5))[-1],
            wait=[("pe", pe_r3)] + fr.get("r1b", []), inc=[("act", 1)])["act"]

        g1t = S3[:, 15360:15872]
        g3t = S3[:, 15872:16384]
        evD = {}
        mmD = {}
        r2mm = None
        prev_ag3 = None
        for q in range(4):
            for jh in range(22):
                u = q * 22 + jh
                b = u % 2
                ww = [("pe", mmD[u - 2])] if u - 2 in mmD \
                    else fr.get("wfb", [])
                q_w1 = st_w1.load(apv(wf1, b * 1024,
                                      [[2048, 128], [128, 8], [1, 128]]),
                                  apv(io["w1_bf"], q * FQ + jh * 128,
                                      [[FH, 128], [128 * FH, 8], [1, 128]]),
                                  wait=ww)
                q_w3 = st_w3.load(apv(wf3, b * 1024,
                                      [[2048, 128], [128, 8], [1, 128]]),
                                  apv(io["w3_bf"], q * FQ + jh * 128,
                                      [[FH, 128], [128 * FH, 8], [1, 128]]))
                wps = [("act", prev_ag3)] if prev_ag3 is not None else []
                jc = q * FQ + jh * 128
                qb1 = st_b1.load(b13[0:2, b * 256:b * 256 + 128],
                                 io["b1_bf"][0:2, jc:jc + 128], wait=ww)
                qb3 = st_b3.load(b13[0:2, b * 256 + 128:b * 256 + 256],
                                 io["b3_bf"][0:2, jc:jc + 128])

                def f_g(e, b=b, jc=jc):
                    last = None
                    for kt in range(8):
                        e.matmul(pD[:, 0:512],
                                 wf1[:, b * 1024 + kt * 128: b * 1024 + (kt + 1) * 128],
                                 S3[:, kt * 512:(kt + 1) * 512],
                                 start=(kt == 0), stop=False)
                    e.matmul(pD[:, 0:512], b13[0:2, b * 256:b * 256 + 128],
                             r1b[0:2, 0:512], start=False, stop=True)
                    for kt in range(8):
                        e.matmul(pD[:, 512:1024],
                                 wf3[:, b * 1024 + kt * 128: b * 1024 + (kt + 1) * 128],
                                 S3[:, kt * 512:(kt + 1) * 512],
                                 start=(kt == 0), stop=False)
                    last = e.matmul(pD[:, 512:1024],
                                    b13[0:2, b * 256 + 128:b * 256 + 256],
                                    r1b[0:2, 512:1024], start=False, stop=True)
                    return last
                wfirst = [("act", a_rr), ("dve", d_xs)] if u == 0 else []
                mmD[u] = p.on("tensor", f_g,
                              wait=[q_w1, q_w3, qb1, qb3] + wps + wfirst,
                              inc=[("pe", 1)])["pe"]
                wgu = [("dve", evD[u - 1])] if u - 1 in evD else []
                a_g1 = p.on("scalar", lambda e: e.activation(
                    g1t, pD[:, 0:512], AF.Silu),
                    wait=[("pe", mmD[u])] + fr.get("g1t", []) + wgu,
                    inc=[("act", 1)])["act"]
                a_g3 = p.on("scalar", lambda e: e.activation(
                    g3t, pD[:, 512:1024], AF.Copy),
                    inc=[("act", 1)])["act"]
                prev_ag3 = a_g3
                d_gu = p.on("vector", lambda e: e.tensor_tensor(
                    g1t, g1t, g3t, ALU.mult),
                    wait=[("act", a_g3)], inc=[("dve", 1)])["dve"]
                fr["g1t"] = [("dve", 0)]
                ys_sl = S3[:, 4096 + jh * 512: 4096 + (jh + 1) * 512]
                if q == 0:
                    po_ys = p.on("gpsimd", lambda e, ys_sl=ys_sl: e.tensor_copy(
                        ys_sl, g1t), wait=[("dve", d_gu)] + fr.get("ysd", []),
                        inc=[("pool", 1)])["pool"]
                else:
                    po_ys = p.on("gpsimd", lambda e, ys_sl=ys_sl: e.tensor_tensor(
                        ys_sl, ys_sl, g1t, ALU.add),
                        wait=[("dve", d_gu)], inc=[("pool", 1)])["pool"]
                r2mm = p.on("tensor", lambda e, u=u: e.matmul(
                    pC[0:2, 0:512], a2g[:, u * 2:(u + 1) * 2], g1t,
                    start=(u == 0), stop=(u == 87)),
                    wait=[("dve", d_gu)] +
                    (fr.get("pC", []) if u == 0 else []),
                    inc=[("pe", 1)])["pe"]
                evD[u] = d_gu
                fr["g1t"] = [("pool", po_ys), ("pe", r2mm)]
        if debug and t4 == 0:
            p.dma("sync", io["dbg_ys"], S3[:, :], "qs",
                  wait=[("pool", po_ys), ("dve", d_xs)])
        a_r2b = p.on("scalar", lambda e: e.activation(
            r1b[0:2, 1024:1536], pC[0:2, 0:512], AF.Copy, scale=0.5),
            wait=[("pe", r2mm)], inc=[("act", 1)])["act"]
        fr["pC"] = [("act", a_r2b)]
        fr["r1b"] = [("pe", r2mm)]

        maybe_bar()
        evE = {}
        mmE = {}
        for jn in range(8):
            for h1t in range(22):
                u = jn * 22 + h1t
                b = u % 2
                ww = [("pe", mmE[("u", u - 2)])] if ("u", u - 2) in mmE \
                    else [("pe", mmB[7 * 64 + 63])]
                q_w = st_wo.load(wob[:, b * 512:(b + 1) * 512],
                                 io["w2_bf"][h1t * 128:(h1t + 1) * 128,
                                             jn * 512:(jn + 1) * 512],
                                 wait=ww + ([("pe", mmB[7 * 64 + 63])]
                                            if jn == 0 and h1t == 0 else []))

                def f_mm(e, h1t=h1t, b=b):
                    last = None
                    for tt in range(4):
                        last = e.matmul(
                            po_tt[tt],
                            S3[:, 4096 + h1t * 512 + tt * 128:
                               4096 + h1t * 512 + tt * 128 + 128],
                            wob[:, b * 512:(b + 1) * 512],
                            start=(h1t == 0), stop=False)
                    return last
                wps = []
                if h1t == 0:
                    wps = [("act", evE[(jn - 1, 3)])] if jn >= 1 \
                        else [("act", evB[(7, 3)]), ("pool", po_ys)]
                mmE[(jn, h1t)] = p.on("tensor", f_mm,
                                      wait=[q_w] + wps,
                                      inc=[("pe", 1)])["pe"]
                mmE[("u", u)] = mmE[(jn, h1t)]

            def f_lora(e, jn=jn):
                last = None
                for tt in range(4):
                    last = e.matmul(po_tt[tt], r1b[0:2, 1024 + tt * 128:
                                                   1024 + (tt + 1) * 128],
                                    b2s[0:2, jn * 512:(jn + 1) * 512],
                                    start=False, stop=True)
                return last
            mmE[(jn, 21)] = p.on("tensor", f_lora, wait=[("act", a_r2b)],
                                 inc=[("pe", 1)])["pe"]
            for tt in range(4):
                o2 = S2[:, tt * 4096:(tt + 1) * 4096]

                def f_ev(e, tt=tt, jn=jn, o2=o2):
                    return e.copy(o2[:, jn * 512:(jn + 1) * 512], po_tt[tt])
                evE[(jn, tt)] = p.on("scalar", f_ev,
                                     wait=[("pe", mmE[(jn, 21)])] +
                                     (fr.get("S2e", []) if jn == 0 else []),
                                     inc=[("act", 1)])["act"]

        maybe_bar()
        q_outs = []
        for tt in range(4):
            o2 = S2[:, tt * 4096:(tt + 1) * 4096]
            x1 = S1[:, tt * 4096:(tt + 1) * 4096]
            ln = emit_rowln(p, o2, cols, c1f,
                            [("act", evE[(7, tt)])] + fr.get("c1f", []) +
                            fr.get("lncols", []))
            fr["lncols"] = [("dve", ln[3]), ("act", ln[2])]
            d_c1 = p.on("vector", lambda e, o2=o2, m=ln[0], r=ln[1]:
                        e.tensor_scalar(c1f[:, :], o2, m, r,
                                        ALU.subtract, ALU.mult),
                        wait=[("act", ln[2])], inc=[("dve", 1)])["dve"]
            d_o = p.on("vector", lambda e, x1=x1: e.tensor_tensor(
                c1f[:, :], c1f[:, :], x1, ALU.add),
                wait=[("dve", d_c1)], inc=[("dve", 1)])["dve"]
            q_o = st_o3.load(io["out_d"][tau + tt * 128: tau + (tt + 1) * 128, :],
                             c1f[:, :], wait=[("dve", d_o)])
            q_outs.append(q_o)
            fr["c1f"] = [q_o]

        maybe_bar()
        fr["wob"] = [("pe", mmE[(7, 21)])]
        fr["wfb"] = [("pe", mmE[(7, 21)])]
        fr["S2a"] = [("dve", p.v("dve"))]
        fr["S3a"] = [("pe", mmE[(7, 21)])]
        fr["S1a"] = [q_outs[-1], ("pe", mmD[87])]
        fr["pAB"] = [("act", evE[(7, 3)])]
        fr["pD"] = [("pe", mmD[87]), ("act", a_g3)]
        fr["ysd"] = [("pe", mmE[(7, 21)])]
        fr["S3d"] = [("pe", mmE[(7, 21)])]
        fr["S2e"] = []
        fr["qout"] = None

    p.on("sync", None, wait=[(s, p.v(s)) for s in p.dma_sems if p.v(s) > 0])


def build_phase3_io(nc):
    io = {}
    EI = "ExternalInput"
    io["x0c"] = nc.dram_tensor("x0c", [TH, DM], F32, kind=EI).ap()
    io["ypart_sp"] = nc.dram_tensor("ypart_sp", [T, INTER], BF16, kind=EI).ap()
    io["gate_sp"] = nc.dram_tensor("gate_sp", [T, INTER], BF16, kind=EI).ap()
    io["bc_sp"] = nc.dram_tensor("bc_sp", [2048, T], BF16, kind=EI).ap()
    io["e3_sp"] = nc.dram_tensor("e3_sp", [NCH, 128, 128], F32, kind=EI).ap()
    io["sin_d"] = nc.dram_tensor("sin_d", [128, INTER], BF16, kind=EI).ap()
    io["mask_d"] = nc.dram_tensor("mask_d", [128, 1], F32, kind=EI).ap()
    io["idbf_d"] = nc.dram_tensor("idbf_d", [128, 128], BF16, kind=EI).ap()
    io["normw_d"] = nc.dram_tensor("normw_d", [128, 64], F32, kind=EI).ap()
    io["WoutT"] = nc.dram_tensor("WoutT", [INTER, DM], F32, kind=EI).ap()
    io["w1qT"] = nc.dram_tensor("w1qT", [1024, FH], F32, kind=EI).ap()
    io["w3qT"] = nc.dram_tensor("w3qT", [1024, FH], F32, kind=EI).ap()
    io["w2qT"] = nc.dram_tensor("w2qT", [FQ, DM], F32, kind=EI).ap()
    io["a1T"] = nc.dram_tensor("a1T", [DM, 2], F32, kind=EI).ap()
    io["a3T"] = nc.dram_tensor("a3T", [DM, 2], F32, kind=EI).ap()
    io["a2T"] = nc.dram_tensor("a2T", [FH, 2], F32, kind=EI).ap()
    io["b1T"] = nc.dram_tensor("b1T", [2, FH], F32, kind=EI).ap()
    io["b3T"] = nc.dram_tensor("b3T", [2, FH], F32, kind=EI).ap()
    io["b2T"] = nc.dram_tensor("b2T", [2, DM], F32, kind=EI).ap()
    io["out_d"] = nc.dram_tensor("out_d", [T, DM], F32, kind="ExternalOutput").ap()
    for nm, shp, dt in (("wout_bf", [INTER, DM], BF16),
                        ("w1_bf", [1024, FH], BF16), ("w3_bf", [1024, FH], BF16),
                        ("w2_bf", [FQ, DM], BF16),
                        ("a1_bf", [DM, 2], BF16), ("a3_bf", [DM, 2], BF16),
                        ("a2_bf", [FH, 2], BF16),
                        ("b1_bf", [2, FH], BF16), ("b3_bf", [2, FH], BF16),
                        ("b2_bf", [2, DM], BF16),
                        ("ycorr", [T, INTER], BF16)):
        io[nm] = nc.dram_tensor(nm, shp, dt, kind="Internal").ap()
    return io


def alloc_prestage_sb(nc, ctx):
    E = ctx.enter_context
    return {"wpre": E(nc.sbuf_tensor("wpre", [128, 8192], BF16)),
            "nwsb": E(nc.sbuf_tensor("nwsb", [128, 64], F32))}


def alloc_phase25_sb(nc, ctx):
    E = ctx.enter_context
    return {"smv": E(nc.sbuf_tensor("smv", [128, INTER], BF16)),
            "ccf": E(nc.sbuf_tensor("ccf", [128, 1024], BF16)),
            "e3t": E(nc.sbuf_tensor("e3t", [128, 256], F32)),
            "ytl": E(nc.sbuf_tensor("ytl", [128, 2 * INTER], BF16)),
            "ufc": E(nc.sbuf_tensor("ufc", [128, 1024], F32)),
            "mcol": E(nc.sbuf_tensor("mcol", [128, 1], F32))}


def alloc_phase3_sb(nc, ctx):
    E = ctx.enter_context
    sb = {}
    sb["S1"] = E(nc.sbuf_tensor("S1", [128, 32768], BF16))
    sb["S2"] = E(nc.sbuf_tensor("S2", [128, 16384], BF16))
    sb["S3"] = E(nc.sbuf_tensor("S3", [128, 16384], BF16))
    sb["xzf"] = E(nc.sbuf_tensor("xzf", [128, 4096], F32))
    sb["c1f"] = E(nc.sbuf_tensor("c1f", [128, 4096], F32))
    sb["wob"] = E(nc.sbuf_tensor("wob", [128, 1024], BF16))
    sb["wf1"] = E(nc.sbuf_tensor("wf1", [128, 2048], BF16))
    sb["wf3"] = E(nc.sbuf_tensor("wf3", [128, 2048], BF16))
    sb["b2s"] = E(nc.sbuf_tensor("b2s", [2, DM], BF16))
    sb["b13"] = E(nc.sbuf_tensor("b13", [2, 512], BF16))
    sb["r1b"] = E(nc.sbuf_tensor("r1b", [2, 1536], BF16))
    sb["a1g"] = E(nc.sbuf_tensor("a1g", [128, 64], BF16))
    sb["a3g"] = E(nc.sbuf_tensor("a3g", [128, 64], BF16))
    sb["a2g"] = E(nc.sbuf_tensor("a2g", [128, 176], BF16))
    sb["cols3"] = E(nc.sbuf_tensor("cols3", [128, 24], F32))
    sb["idb3"] = E(nc.sbuf_tensor("idb3", [128, 128], BF16))
    return sb


def alloc_phase3_ps(nc, ctx):
    E = ctx.enter_context
    ps = {}
    ps["pA"] = E(nc.psum_tensor("pA", [128, 1024], F32))
    ps["pB"] = E(nc.psum_tensor("pB", [128, 1024], F32))
    ps["pD"] = E(nc.psum_tensor("pD", [128, 1024], F32))
    ps["pC"] = E(nc.psum_tensor("pC", [128, 512], F32))
    ps["pT"] = E(nc.psum_tensor("pT", [128, 1024], BF16))
    ps["pQ"] = [ps["pA"], ps["pB"]]
    return ps


def build_phase3(debug=False):
    nc = bass.Bass("TRN2", target_bir_lowering=False)
    io = build_phase3_io(nc)
    if debug:
        io["ycorr"] = nc.dram_tensor("ycorr_o", [T, INTER], BF16,
                                     kind="ExternalOutput").ap()
        io["dbg_ygn"] = nc.dram_tensor("dbg_ygn", [128, 32768], BF16,
                                       kind="ExternalOutput").ap()
        io["dbg_obf"] = nc.dram_tensor("dbg_obf", [128, 16384], BF16,
                                       kind="ExternalOutput").ap()
        io["dbg_x1"] = nc.dram_tensor("dbg_x1", [128, 32768], BF16,
                                      kind="ExternalOutput").ap()
        io["dbg_ys"] = nc.dram_tensor("dbg_ys", [128, 16384], BF16,
                                      kind="ExternalOutput").ap()
    with ExitStack() as ctx:
        p = Prog()
        p.sem("qs", "qg", "pe", "dve", "act", "pool", "bar")
        ps = alloc_phase3_ps(nc, ctx)
        with ExitStack() as c0:
            sbp = alloc_prestage_sb(nc, c0)
            emit_prestage(p, io, sbp)
        with ExitStack() as c1:
            sb25 = alloc_phase25_sb(nc, c1)
            q_corr = emit_phase25(p, io, sb25, ps)
        sb = alloc_phase3_sb(nc, ctx)
        emit_phase3(p, io, sb, ps, q_corr, debug=debug)
        sems = {n: ctx.enter_context(nc.semaphore(n)) for n in p.cnt}
        block = ctx.enter_context(nc.Block())
        p.install(block, sems)
    return nc


def host_layout_phase3(inputs, p1outs, p2outs):
    x0 = np.asarray(inputs["L0_attn_hidden_states"], np.float32)

    def padh(w):
        w = np.asarray(w, np.float32)
        q = w.reshape(4, 2752, -1)
        out = np.zeros((4, FQ, w.shape[1]), np.float32)
        out[:, :2752] = q
        return out.reshape(FH, w.shape[1])

    WoutT = np.ascontiguousarray(np.asarray(inputs["L0_attn_out_proj_w"],
                                            np.float32).T)
    normw = np.ascontiguousarray(np.asarray(inputs["L0_attn_norm_w"], np.float32)
                                 .reshape(64, 128).T)
    w1qT = np.ascontiguousarray(padh(inputs["L0_ffn_w1q"]).T)
    w3qT = np.ascontiguousarray(padh(inputs["L0_ffn_w3q"]).T)
    w2qT = np.zeros((FQ, DM), np.float32)
    w2qT[:2752] = np.asarray(inputs["L0_ffn_w2q"], np.float32).T
    a1T = np.ascontiguousarray(np.asarray(inputs["L0_ffn_a1"], np.float32).T)
    a3T = np.ascontiguousarray(np.asarray(inputs["L0_ffn_a3"], np.float32).T)
    a2T = np.ascontiguousarray(padh(np.ascontiguousarray(
        np.asarray(inputs["L0_ffn_a2"], np.float32).T)))
    b1T = np.ascontiguousarray(padh(inputs["L0_ffn_b1"]).T)
    b3T = np.ascontiguousarray(padh(inputs["L0_ffn_b3"]).T)
    b2T = np.ascontiguousarray(np.asarray(inputs["L0_ffn_b2"], np.float32).T)
    idbf = np.eye(128, dtype=np.float32).astype(NPBF)
    maps = []
    for core in range(NCORES):
        b, s = core // 2, core % 2
        shard = x0[b, s * T:(s + 1) * T]
        halo = np.zeros((3, DM), np.float32) if s == 0 else x0[b, s * T - 3: s * T]
        x0c = np.ascontiguousarray(np.concatenate([halo, shard], 0))
        if s == 1:
            sin = np.asarray(p2outs[core - 1]["slo_sp"])
            mask = np.full((128, 1), 1.0, np.float32)
        else:
            sin = np.zeros((128, INTER), NPBF)
            mask = np.zeros((128, 1), np.float32)
        maps.append({
            "x0c": x0c, "ypart_sp": p2outs[core]["ypart_sp"],
            "gate_sp": p1outs[core]["gate_sp"], "bc_sp": p1outs[core]["bc_sp"],
            "e3_sp": np.asarray(p2outs[core]["e3_sp"], np.float32),
            "sin_d": sin, "mask_d": mask, "idbf_d": idbf, "normw_d": normw,
            "WoutT": WoutT, "w1qT": w1qT, "w3qT": w3qT, "w2qT": w2qT,
            "a1T": a1T, "a3T": a3T, "a2T": a2T, "b1T": b1T, "b3T": b3T,
            "b2T": b2T})
    return maps



def _padh(w):
    w = np.asarray(w)
    q = w.reshape(4, 2752, -1)
    out = np.zeros((4, FQ, w.shape[1]), w.dtype)
    out[:, :2752] = q
    return out.reshape(FH, w.shape[1])


def host_layout_phase1(inputs):
    x0 = np.asarray(inputs["L0_attn_hidden_states"], np.float32)
    WinT = np.ascontiguousarray(np.asarray(inputs["L0_attn_in_proj_w"], np.float32).T)
    cw = np.asarray(inputs["L0_attn_conv_w"], np.float32).reshape(CONV, 4)
    cw_d = np.ascontiguousarray(cw.reshape(80, 128, 4).transpose(1, 0, 2).reshape(128, 320))
    cb_d = np.ascontiguousarray(np.asarray(inputs["L0_attn_conv_b"], np.float32)
                                .reshape(80, 128).T)
    dtb_d = np.asarray(inputs["L0_attn_dt_bias"], np.float32).reshape(128, 1)
    idbf_d = np.eye(128, dtype=np.float32).astype(NPBF)
    maps = []
    for core in range(NCORES):
        b, s = core // 2, core % 2
        shard = x0[b, s * T:(s + 1) * T]
        halo = np.zeros((3, DM), np.float32) if s == 0 else x0[b, s * T - 3: s * T]
        x0c = np.ascontiguousarray(np.concatenate([halo, shard], 0))
        maps.append({"x0c": x0c, "WinT": WinT, "cw_d": cw_d, "cb_d": cb_d,
                     "dtb_d": dtb_d, "idbf_d": idbf_d})
    return maps


_NC_CACHE = {}


def _get_nc(name, builder):
    if name not in _NC_CACHE:
        _NC_CACHE[name] = builder()
    return _NC_CACHE[name]


def kernel(**inputs):
    cores = list(range(NCORES))
    maps1 = host_layout_phase1(inputs)
    r1 = run_bass_kernel_spmd(_get_nc("p1", build_phase1), maps1, core_ids=cores)
    p1o = r1.results
    maps2 = host_layout_phase2(inputs, p1o)
    r2 = run_bass_kernel_spmd(_get_nc("p2", build_phase2), maps2, core_ids=cores)
    p2o = r2.results
    maps3 = host_layout_phase3(inputs, p1o, p2o)
    r3 = run_bass_kernel_spmd(_get_nc("p3", build_phase3), maps3, core_ids=cores)
    out = np.zeros((BATCH, SEQ, DM), np.float32)
    for core in cores:
        b, s = core // 2, core % 2
        out[b, s * T:(s + 1) * T] = r3.results[core]["out_d"]
    return out


# version 03
# speedup vs baseline: 1.0233x; 1.0233x over previous
from contextlib import ExitStack

import numpy as np
import ml_dtypes
import concourse.bass as bass
import concourse.mybir as mybir
from concourse.bass_utils import run_bass_kernel_spmd

F32 = mybir.dt.float32
BF16 = mybir.dt.bfloat16
AF = mybir.ActivationFunctionType
ALU = mybir.AluOpType
AX = mybir.AxisListType
NPBF = np.dtype(ml_dtypes.bfloat16)

NCORES = 8
BATCH, SEQ, DM = 4, 4096, 4096
T = 2048
TH = T + 3
NH, HD, SS, NG = 128, 64, 128, 8
INTER = NH * HD
CONV = INTER + 2 * NG * SS
PROJ = INTER + CONV + NH
CHUNK = 128
NCH = T // CHUNK
EPS = 1e-5
GROUP = INTER // NG
FH = 11264
FQ = 2816

ENGS = ("sync", "tensor", "vector", "scalar", "gpsimd")


class Prog:

    def __init__(self):
        self.ops = {e: [] for e in ENGS}
        self.cnt = {}
        self.dma_sems = set()

    def sem(self, *names):
        for n in names:
            self.cnt[n] = 0

    def on(self, eng, fn=None, wait=(), inc=()):
        for s, n in inc:
            self.cnt[s] += n
        after = {s: self.cnt[s] for s, _ in inc}
        self.ops[eng].append((fn, list(wait), list(inc)))
        return after

    def v(self, s):
        return self.cnt[s]

    def stream(self, name, eng="sync", nbuf=2):
        return Stream(self, name, eng, nbuf)

    def barrier_all(self):
        for e in ENGS:
            self.on(e, lambda en: en.nop(), inc=[("bar", 1)])
        target = self.v("bar")
        qw = [(s, self.v(s)) for s in self.cnt
              if s.startswith("q") and self.v(s) > 0]
        for e in ENGS:
            self.on(e, None, wait=[("bar", target)] + qw)

    def quiesce(self):
        w = [(s, self.v(s)) for s in self.dma_sems if self.v(s) > 0]
        return self.on("sync", lambda e: e.nop(), wait=w,
                       inc=[("bar", 1)])["bar"]

    def dma(self, eng, out, in_, sem, wait=(), **kw):
        self.dma_sems.add(sem)
        return self.on(eng, lambda e: e.dma_start(out=out, in_=in_, **kw),
                       wait=wait, inc=[(sem, 16)])[sem]

    def install(self, block, sems):
        for eng_name in ENGS:
            ops = self.ops[eng_name]
            if not ops:
                continue

            def body(engine, ops=ops):
                last_wait = {}
                for fn, wait, inc in ops:
                    for s, val in wait:
                        if last_wait.get(s, -1) >= val:
                            continue
                        engine.wait_ge(sems[s], val)
                        last_wait[s] = val
                    inst = fn(engine) if fn is not None else None
                    if inc:
                        assert inst is not None, "inc requires an instruction"
                        for s, n in inc:
                            inst.then_inc(sems[s], n)
            getattr(block, eng_name)(body)


class Stream:

    def __init__(self, p, name, eng="sync", nbuf=2):
        self.p, self.name, self.eng, self.nbuf = p, name, eng, nbuf
        p.sem(*[f"{name}{i}" for i in range(nbuf)])
        self.i = 0

    def load(self, out, in_, wait=(), **kw):
        s = f"{self.name}{self.i % self.nbuf}"
        v = self.p.dma(self.eng, out, in_, s, wait=wait, **kw)
        self.i += 1
        return (s, v)

    def sems(self):
        return [f"{self.name}{i}" for i in range(self.nbuf)]


def apv(t, offset, dims):
    if isinstance(t, bass.AP):
        return bass.AP(t.tensor, offset, [list(d) for d in dims])
    a = t.ap()
    return bass.AP(a.tensor, offset, [list(d) for d in dims])


def rowstep(t):
    return t.ap().ap[0][0]



def emit_phase1(p, io, sb, ps):
    x0c, WinT = io["x0c"], io["WinT"]
    gate_sp, xtm_sp, bc_sp, dt_sp = io["gate_sp"], io["xtm_sp"], io["bc_sp"], io["dt_sp"]

    h1f = sb["h1f"]
    xld = sb["xld"]
    h1t = sb["h1t"]
    wta = sb["wta"]
    xcb = sb["xcb"]
    cacc = sb["cacc"]
    cacd = sb["cacd"]
    xcv = sb["xcv"]
    xT = sb["xT"]
    cwsb, cbsb = sb["cwsb"], sb["cbsb"]
    dtb = sb["dtb"]
    idbf = sb["idbf"]
    cols = sb["cols"]
    pmm = ps["pmm"]
    ptr = ps["ptr"]

    HROW = rowstep(h1f)
    WROW = rowstep(wta)
    XROW = rowstep(xT)

    p.dma("sync", sb["cwsb"][:, :], io["cw_d"], "qs")
    p.dma("sync", sb["cbsb"][:, :], io["cb_d"], "qs")
    p.dma("sync", sb["dtb"][:, :], io["dtb_d"], "qs")
    qs_setup = p.dma("sync", sb["idbf"][:, :], io["idbf_d"], "qs")
    bar0 = p.quiesce()

    dve_h1 = {}
    tb_dve = {}
    tb = 0
    pe_trans_last = None
    st_x = p.stream("x1l", "sync", 1)
    st_w = p.stream("w1t", "gpsimd", 2)
    st_xT = p.stream("s1x", "sync", 1)
    st_bc = p.stream("s1b", "sync", 1)
    for tt in range(17):
        rows = 128 if tt < 16 else 3
        xv = xld[0:rows, 0:4096]
        ww = [("dve", dve_h1[tt - 1])] if tt - 1 in dve_h1 else []
        qs_x = st_x.load(xv, x0c[tt * 128: tt * 128 + rows, :],
                         wait=ww + [("bar", bar0)])
        (c_sum, c_mean, c_sqa, c_sqb, c_sq, c_m2, c_var, c_rstd) = (
            cols[0:rows, i:i + 1] for i in range(8))

        def a_sq(e, xv=xv, c_sqa=c_sqa, c_sqb=c_sqb, rows=rows):
            e.activation(cacc[0:rows, 0:2048], xv[:, 0:2048], AF.Square,
                         accum_out=c_sqa)
            return e.activation(cacd[0:rows, 0:2048], xv[:, 2048:4096], AF.Square,
                                accum_out=c_sqb)
        act_sq = p.on("scalar", a_sq, wait=[qs_x], inc=[("act", 1)])["act"]

        def v_sum(e, xv=xv, c_sum=c_sum):
            return e.tensor_reduce(c_sum, xv, AX.X, ALU.add)
        da = p.on("vector", v_sum, wait=[qs_x], inc=[("dve", 1)])["dve"]
        db = p.on("vector", lambda e, a=c_mean, bb=c_sum: e.tensor_scalar_mul(
            a, bb, 1.0 / DM), wait=[("dve", da)], inc=[("dve", 1)])["dve"]
        dc = p.on("vector", lambda e, a=c_m2, bb=c_mean: e.tensor_tensor(
            a, bb, bb, ALU.mult), wait=[("dve", db)], inc=[("dve", 1)])["dve"]
        dd = p.on("vector", lambda e, a=c_sq, x1=c_sqa, x2=c_sqb: e.tensor_tensor(
            a, x1, x2, ALU.add), wait=[("act", act_sq)], inc=[("dve", 1)])["dve"]
        de = p.on("vector", lambda e, a=c_var, bb=c_sq: e.tensor_scalar(
            a, bb, 1.0 / DM, EPS, ALU.mult, ALU.add),
            wait=[("dve", dd)], inc=[("dve", 1)])["dve"]
        dve_st = p.on("vector", lambda e, a=c_var, bb=c_m2: e.tensor_tensor(
            a, a, bb, ALU.subtract), wait=[("dve", de)], inc=[("dve", 1)])["dve"]

        a_ln = p.on("scalar", lambda e, a=c_var: e.activation(a, a, AF.Ln),
                    wait=[("dve", dve_st)], inc=[("act", 1)])["act"]
        act_r = p.on("scalar", lambda e, a=c_rstd, bb=c_var: e.activation(
            a, bb, AF.Exp, scale=-0.5), wait=[("act", a_ln)],
            inc=[("act", 1)])["act"]

        wh = [("pe", pe_trans_last)] if pe_trans_last is not None else []
        dve_h1[tt] = p.on("vector", lambda e, xv=xv, rows=rows, m=c_mean, r=c_rstd:
                          e.tensor_scalar(h1t[0:rows, :], xv, m, r,
                                          ALU.subtract, ALU.mult),
                          wait=[("act", act_r)] + wh, inc=[("dve", 1)])["dve"]

        for bt in range(4):
            pb = tb % 2
            wtb = [("act", tb_dve[tb - 2])] if tb - 2 in tb_dve else []

            def f_tr(e, bt=bt, rows=rows, pb=pb):
                last = None
                for i in range(8):
                    kt = bt * 8 + i
                    last = e.transpose(ptr[pb][:, i * 128: i * 128 + rows],
                                       h1t[0:rows, kt * 128:(kt + 1) * 128],
                                       idbf[0:rows, 0:rows])
                return last
            pe_tr = p.on("tensor", f_tr, wait=[("dve", dve_h1[tt])] + wtb,
                         inc=[("pe", 1)])["pe"]
            pe_trans_last = pe_tr

            def f_ev(e, bt=bt, rows=rows, pb=pb, tt=tt):
                src = apv(ptr[pb], 0, [[1024, 128], [128, 8], [1, rows]])
                dst = apv(h1f, (bt * 8) * 2051 + tt * 128,
                          [[HROW, 128], [2051, 8], [1, rows]])
                return e.copy(dst, src)
            tb_dve[tb] = p.on("scalar", f_ev, wait=[("pe", pe_tr)],
                              inc=[("act", 1)])["act"]
            tb += 1

    stepA_done = tb_dve[tb - 1]
    if "h1dbg" in io:
        p.dma("sync", io["h1dbg"], h1f[:, :], "qs", wait=[("act", stepA_done)])

    NS = [(0, 512), (512, 512), (1024, 512), (1536, 512), (2048, 3)]
    pe_mm = {}
    act_ev = {}
    xcb_free = {}
    conv_free = {}
    u = 0
    last_xT = None
    last_dve_c = None
    last_act_s = None

    for jb in range(145):
        b = jb % 2
        ww = [("pe", pe_mm[(jb - 2, 4)])] if (jb - 2, 4) in pe_mm else []
        src = apv(WinT, jb * 128, [[PROJ, 128], [128 * PROJ, 32], [1, 128]])
        dst = apv(wta, b * 4096, [[WROW, 128], [128, 32], [1, 128]])
        qg_w = st_w.load(dst, src, wait=ww)

        for ns in range(5):
            n0, nn = NS[ns]
            bank = u % 4
            wb = [("act", act_ev[u - 4])] if u - 4 in act_ev else []

            def f_mm(e, b=b, n0=n0, nn=nn, bank=bank):
                last = None
                for kt in range(32):
                    last = e.matmul(
                        pmm[bank][:, 0:nn],
                        wta[:, b * 4096 + kt * 128: b * 4096 + (kt + 1) * 128],
                        h1f[:, kt * 2051 + n0: kt * 2051 + n0 + nn],
                        start=(kt == 0), stop=(kt == 31))
                return last
            pe_mm[(jb, ns)] = p.on("tensor", f_mm,
                                   wait=[qg_w, ("act", stepA_done)] + wb,
                                   inc=[("pe", 1)])["pe"]
            if jb < 144:
                wx = xcb_free.get(jb - 2, [])

                def f_ev(e, b=b, n0=n0, nn=nn, bank=bank):
                    return e.copy(xcb[:, b * 2051 + n0: b * 2051 + n0 + nn],
                                  pmm[bank][:, 0:nn])
            else:
                wx = [("dve", last_dve_c)] if last_dve_c is not None else []

                def f_ev(e, n0=n0, nn=nn, bank=bank):
                    return e.copy(cacc[:, n0:n0 + nn], pmm[bank][:, 0:nn])
            act_ev[u] = p.on("scalar", f_ev,
                             wait=[("pe", pe_mm[(jb, ns)])] + wx,
                             inc=[("act", 1)])["act"]
            u += 1
        evac_all = act_ev[u - 1]

        def spill_transposed(src_sb, src_off, dst_dram, dst_col, wait_pre):
            nonlocal tb, pe_trans_last, last_xT
            pe_last = None
            for bt in range(2):
                pb = tb % 2
                wtb = [("act", tb_dve[tb - 2])] if tb - 2 in tb_dve else []

                def f_tr(e, bt=bt, pb=pb, src_sb=src_sb, src_off=src_off):
                    last = None
                    for i in range(8):
                        ti = bt * 8 + i
                        last = e.transpose(
                            ptr[pb][:, i * 128:(i + 1) * 128],
                            src_sb[:, src_off + ti * 128: src_off + (ti + 1) * 128],
                            idbf[:, :])
                    return last
                pe_tr = p.on("tensor", f_tr, wait=wait_pre + wtb,
                             inc=[("pe", 1)])["pe"]
                pe_last = pe_tr
                wxT = [last_xT] if last_xT is not None else []

                def f_ev(e, bt=bt, pb=pb):
                    src = apv(ptr[pb], 0, [[1024, 128], [128, 8], [1, 128]])
                    dst = apv(xT, bt * 1024, [[XROW, 128], [128, 8], [1, 128]])
                    return e.copy(dst, src)
                tb_dve[tb] = p.on("scalar", f_ev, wait=[("pe", pe_tr)] + wxT,
                                  inc=[("act", 1)])["act"]
                tb += 1
            src = apv(xT, 0, [[XROW, 128], [128, 16], [1, 128]])
            dst = apv(dst_dram, dst_col,
                      [[INTER, 128], [128 * INTER, 16], [1, 128]])
            last_xT = st_xT.load(dst, src, wait=[("act", tb_dve[tb - 1])])
            pe_trans_last = pe_last
            return pe_last

        if jb < 64:
            pe_last = spill_transposed(xcb, b * 2051 + 3, gate_sp, jb * 128,
                                       [("act", evac_all)])
            xcb_free[jb] = [("pe", pe_last)]
        elif jb < 144:
            cj = jb - 64
            w_cacd = [("act", last_act_s)] if last_act_s is not None else []

            def f_cd(e, b=b, cj=cj):
                e.tensor_scalar(cacd[:, 0:2048],
                                xcb[:, b * 2051: b * 2051 + 2048],
                                cwsb[:, cj * 4: cj * 4 + 1], None, ALU.mult)
                inst = None
                for k in (1, 2, 3):
                    inst = e.scalar_tensor_tensor(
                        cacd[:, 0:2048],
                        xcb[:, b * 2051 + k: b * 2051 + k + 2048],
                        cwsb[:, cj * 4 + k: cj * 4 + k + 1],
                        cacd[:, 0:2048],
                        ALU.mult, ALU.add)
                return inst
            dve_c = p.on("vector", f_cd, wait=[("act", evac_all)] + w_cacd,
                         inc=[("dve", 1)])["dve"]
            last_dve_c = dve_c

            w_xcv = conv_free.get(jb - 1, [])

            def f_si(e, cj=cj):
                return e.activation(xcv[:, 0:2048], cacd[:, 0:2048],
                                    AF.Silu, bias=cbsb[:, cj:cj + 1])
            act_s = p.on("scalar", f_si, wait=[("dve", dve_c)] + w_xcv,
                         inc=[("act", 1)])["act"]
            last_act_s = act_s
            xcb_free[jb] = [("dve", dve_c)]

            if jb < 128:
                pe_last = spill_transposed(xcv, 0, xtm_sp, (jb - 64) * 128,
                                           [("act", act_s)])
                conv_free[jb] = [("pe", pe_last)]
            else:
                qd = st_bc.load(bc_sp[(jb - 128) * 128:(jb - 127) * 128, :],
                                xcv[:, 0:2048], wait=[("act", act_s)])
                conv_free[jb] = [qd]
        else:
            raw = cacc[:, 0:2051]
            aa = cacd[:, 0:2051]
            tt_ = xld[:, 0:2051]
            w_sil = [("act", last_act_s)] if last_act_s is not None else []
            d0 = p.on("vector", lambda e: e.tensor_scalar(
                raw, raw, dtb[:, 0:1], None, ALU.add), wait=[("act", evac_all)],
                inc=[("dve", 1)])["dve"]
            a_abs = p.on("scalar", lambda e: e.activation(aa, raw, AF.Abs),
                         wait=[("dve", d0)] + w_sil, inc=[("act", 1)])["act"]
            a0 = p.on("scalar", lambda e: e.activation(tt_, aa, AF.Exp, scale=-1.0),
                      inc=[("act", 1)])["act"]
            d2 = p.on("vector", lambda e: e.tensor_scalar_add(tt_, tt_, 1.0),
                      wait=[("act", a0)], inc=[("dve", 1)])["dve"]
            a1 = p.on("scalar", lambda e: e.activation(aa, tt_, AF.Ln),
                      wait=[("dve", d2)], inc=[("act", 1)])["act"]
            p.on("vector", lambda e: e.tensor_scalar(
                tt_, raw, 0.0, None, ALU.max), wait=[("act", a1)])
            p.on("vector", lambda e: e.tensor_tensor(raw, tt_, aa, ALU.add))
            d3 = p.on("vector", lambda e: e.tensor_scalar(
                tt_, raw, 0.001, 100.0, ALU.max, ALU.min), inc=[("dve", 1)])["dve"]
            p.dma("sync", dt_sp[:, :], tt_[:, 3:2051], "qs", wait=[("dve", d3)])

    p.on("sync", None, wait=[(s, p.v(s)) for s in p.dma_sems if p.v(s) > 0])


def build_phase1_io(nc, internal=False):
    kin = "Internal" if internal else "ExternalInput"
    kout = "Internal" if internal else "ExternalOutput"
    io = {}
    io["x0c"] = nc.dram_tensor("x0c", [TH, DM], F32, kind="ExternalInput").ap()
    io["WinT"] = nc.dram_tensor("WinT", [DM, PROJ], F32, kind="ExternalInput").ap()
    io["cw_d"] = nc.dram_tensor("cw_d", [128, 320], F32, kind="ExternalInput").ap()
    io["cb_d"] = nc.dram_tensor("cb_d", [128, 80], F32, kind="ExternalInput").ap()
    io["dtb_d"] = nc.dram_tensor("dtb_d", [128, 1], F32, kind="ExternalInput").ap()
    io["idbf_d"] = nc.dram_tensor("idbf_d", [128, 128], BF16, kind="ExternalInput").ap()
    io["gate_sp"] = nc.dram_tensor("gate_sp", [T, INTER], BF16, kind=kout).ap()
    io["xtm_sp"] = nc.dram_tensor("xtm_sp", [T, INTER], BF16, kind=kout).ap()
    io["bc_sp"] = nc.dram_tensor("bc_sp", [2048, T], BF16, kind=kout).ap()
    io["dt_sp"] = nc.dram_tensor("dt_sp", [128, T], F32, kind=kout).ap()
    return io


def alloc_phase1_sb(nc, ctx):
    E = ctx.enter_context
    sb = {}
    sb["h1f"] = E(nc.sbuf_tensor("h1f", [128, 32 * 2051], BF16))
    sb["xld"] = E(nc.sbuf_tensor("xld", [128, 4096], F32))
    sb["h1t"] = E(nc.sbuf_tensor("h1t", [128, 4096], BF16))
    sb["wta"] = E(nc.sbuf_tensor("wta", [128, 8192], BF16))
    sb["xcb"] = E(nc.sbuf_tensor("xcb", [128, 2 * 2051], BF16))
    sb["cacc"] = E(nc.sbuf_tensor("cacc", [128, 2051], F32))
    sb["cacd"] = E(nc.sbuf_tensor("cacd", [128, 2051], F32))
    sb["xcv"] = E(nc.sbuf_tensor("xcv", [128, 2048], BF16))
    sb["xT"] = E(nc.sbuf_tensor("xT", [128, 2048], BF16))
    sb["cwsb"] = E(nc.sbuf_tensor("cwsb", [128, 320], F32))
    sb["cbsb"] = E(nc.sbuf_tensor("cbsb", [128, 80], F32))
    sb["dtb"] = E(nc.sbuf_tensor("dtb", [128, 1], F32))
    sb["idbf"] = E(nc.sbuf_tensor("idbf", [128, 128], BF16))
    sb["cols"] = E(nc.sbuf_tensor("cols", [128, 16], F32))
    return sb


def alloc_phase1_ps(nc, ctx):
    E = ctx.enter_context
    ps = {}
    ps["pmm"] = [E(nc.psum_tensor(f"pmm{i}", [128, 512], F32)) for i in range(4)]
    ps["ptr"] = [E(nc.psum_tensor(f"ptr{i}", [128, 1024], BF16)) for i in range(2)]
    return ps


def build_phase1(debug=False):
    nc = bass.Bass("TRN2", target_bir_lowering=False)
    io = build_phase1_io(nc)
    if debug:
        io["h1dbg"] = nc.dram_tensor("h1dbg", [128, 32 * 2051], BF16,
                                     kind="ExternalOutput").ap()
    with ExitStack() as ctx:
        sb = alloc_phase1_sb(nc, ctx)
        ps = alloc_phase1_ps(nc, ctx)
        p = Prog()
        p.sem("qs", "qg", "pe", "dve", "act", "pool", "bar")
        emit_phase1(p, io, sb, ps)
        sems = {n: ctx.enter_context(nc.semaphore(n)) for n in p.cnt}
        block = ctx.enter_context(nc.Block())
        p.install(block, sems)
    return nc



def emit_phase2(p, io, sb, ps):
    xtm_sp, bc_sp, dt_sp = io["xtm_sp"], io["bc_sp"], io["dt_sp"]
    ypart_sp, slo_sp = io["ypart_sp"], io["slo_sp"]
    e3_sp, expcs_sp = io["e3_sp"], io["expcs_sp"]
    csfl, cdrow_d, cumrow_d = io["csfl"], io["cdrow_d"], io["cumrow_d"]

    xtc = sb["xtc"]; xdt = sb["xdt"]; xdtd = sb["xdtd"]; xD = sb["xD"]
    carry = sb["carry"]; carry_bf = sb["carry_bf"]; ybf = sb["ybf"]
    bcsl = sb["bcsl"]; e_sb = sb["e_sb"]; sc_sb = sb["sc_sb"]
    t1s = sb["t1s"]; t2s = sb["t2s"]
    bfm = sb["bfm"]; cfm = sb["cfm"]; btm = sb["btm"]
    chk = sb["chk"]
    dsb = sb["dsb"]; trl = sb["trl"]; idb = sb["idb"]; idf = sb["idf"]
    acol = sb["acol"]; cbm = sb["cbm"]; zz = sb["zz"]
    cdx = sb["cdx"]; cdm = sb["cdm"]; e3s = sb["e3s"]; crow = sb["crow"]
    dtT = sb["dtT"]; decT = sb["decT"]

    pY, pX = ps["pY"], ps["pX"]
    pcb, ptf, ptb = ps["pcb"], ps["ptf"], ps["ptb"]

    CROW = rowstep(chk)
    dtc = chk[:, 0:128]
    dA = chk[:, 128:256]
    cs = chk[:, 256:384]
    csT = chk[:, 384:512]
    expT = chk[:, 512:640]
    dec = chk[:, 640:768]
    ccol = chk[:, 768:769]
    cdc = chk[:, 769:770]

    p.dma("sync", dsb[:, :], apv(io["drow_d"], 0, [[0, 128], [1, 128]]), "qs")
    p.dma("sync", trl[:, :], io["tril_d"], "qs")
    p.dma("sync", idb[:, :], io["idbf_d"], "qs")
    p.dma("sync", idf[:, :], io["idf32_d"], "qs")
    qs_set = p.dma("sync", acol[:, :], io["acol_d"], "qs")
    bar0 = p.quiesce()
    p.on("vector", lambda e: e.memset(carry[:, :], 0.0))
    p.on("vector", lambda e: e.memset(carry_bf[:, :], 0.0))
    p.on("vector", lambda e: e.memset(zz[:, :], 0.0))
    d_init = p.on("vector", lambda e: e.memset(chk[:, 768:770], 0.0),
                  inc=[("dve", 1)])["dve"]
    setup_dve = d_init

    gs = {}
    chk_free = []
    xfree = {}
    prev_spill = None
    cbf_prev = {}
    st_x = p.stream("p2x", "sync", 1)
    st_b = p.stream("p2b", "sync", 1)
    st_c = p.stream("p2c", "sync", 1)
    st_d = p.stream("p2d", "sync", 1)
    st_cs = p.stream("p2s", "sync", 1)
    st_bl = p.stream("p2l", "sync", 2)
    st_r = p.stream("p2r", "sync", 1)
    st_e = p.stream("p2e", "sync", 1)
    st_u = p.stream("p2u", "sync", 1)
    st_m = p.stream("p2m", "sync", 1)
    st_f = p.stream("p2f", "sync", 1)
    st_y = p.stream("p2y", "sync", 1)

    for c in range(NCH):
        qx = st_x.load(xtc[:, :], xtm_sp[c * 128:(c + 1) * 128, :],
                       wait=xfree.get("x", []) + [("bar", bar0)])
        qb = st_b.load(apv(bfm, 0, [[1024, 128], [128, 8], [1, 128]]),
                       apv(bc_sp, c * 128, [[T, 128], [128 * T, 8], [1, 128]]),
                       wait=xfree.get("bc", []))
        qc = st_c.load(apv(cfm, 0, [[1024, 128], [128, 8], [1, 128]]),
                       apv(bc_sp, 1024 * T + c * 128,
                           [[T, 128], [128 * T, 8], [1, 128]]))
        qd = st_d.load(dtc, dt_sp[:, c * 128:(c + 1) * 128],
                       wait=xfree.get("dt", []))

        d1 = p.on("vector", lambda e: e.tensor_scalar(
            dA, dtc, acol[:, 0:1], None, ALU.mult),
            wait=[qd, ("dve", d_init)] + chk_free,
            inc=[("dve", 1)])["dve"]
        d2 = p.on("vector", lambda e: e.tensor_tensor_scan(
            cs, dA, zz[:, 0:128], 0.0, ALU.add, ALU.add),
            wait=[("dve", d1)], inc=[("dve", 1)])["dve"]
        petr1 = p.on("tensor", lambda e: e.transpose(ptf[:, 0:128], cs, idf[:, :]),
                     wait=[("dve", d2)], inc=[("pe", 1)])["pe"]

        def ev_csT(e):
            e.copy(csT, ptf[:, 0:128])
            return e.activation(expT, ptf[:, 0:128], AF.Exp)
        a_csT = p.on("scalar", ev_csT, wait=[("pe", petr1)], inc=[("act", 1)])["act"]
        a_dec = p.on("scalar", lambda e: e.activation(
            dec, cs, AF.Exp, scale=-1.0, bias=cs[:, 127:128]),
            wait=[("dve", d2)], inc=[("act", 1)])["act"]
        a_cd = p.on("scalar", lambda e: e.activation(
            cdc, cs[:, 127:128], AF.Exp), wait=[("act", a_dec)],
            inc=[("act", 1)])["act"]
        petr2 = p.on("tensor", lambda e: e.transpose(ptf[:, 128:256], dtc, idf[:, :]),
                     wait=[qd, ("act", a_csT)], inc=[("pe", 1)])["pe"]
        a_dtT = p.on("scalar", lambda e: e.copy(dtT[:, :], ptf[:, 128:256]),
                     wait=[("pe", petr2)] + xfree.get("dtT", []),
                     inc=[("act", 1)])["act"]
        petr3 = p.on("tensor", lambda e: e.transpose(ptf[:, 256:384], dec, idf[:, :]),
                     wait=[("act", a_dtT)], inc=[("pe", 1)])["pe"]
        a_decT = p.on("scalar", lambda e: e.copy(decT[:, :], ptf[:, 256:384]),
                      wait=[("pe", petr3)] + xfree.get("decT", []),
                      inc=[("act", 1)])["act"]
        petr4 = p.on("tensor", lambda e: e.transpose(
            ptf[0:1, 384:512], cdc, idf[:, :]),
            wait=[("act", a_decT)], inc=[("pe", 1)])["pe"]
        a_crow = p.on("scalar", lambda e: e.copy(
            crow[0:1, 0:128], apv(ptf, 384, [[512, 1], [1, 128]])),
            wait=[("pe", petr4)] + xfree.get("crow", []),
            inc=[("act", 1)])["act"]

        q_csf = st_cs.load(apv(csfl, c * 16384, [[128, 128], [1, 128]]),
                           cs, wait=[("act", a_csT)] + xfree.get("csf", []))
        q_cdr = st_r.load(cdrow_d[c:c + 1, :], crow[0:1, 0:128],
                          wait=[("act", a_crow)])
        q_cdx = st_e.load(cdx[:, :],
                          apv(cdrow_d, c * 128, [[0, 128], [1, 128]]),
                          wait=[q_cdr] + xfree.get("cdx", []))
        petr5 = p.on("tensor", lambda e: e.transpose(
            ptf[0:1, 384:512], ccol, idf[:, :]),
            wait=[("act", a_crow), ("dve", d_init)], inc=[("pe", 1)])["pe"]
        a_cum = p.on("scalar", lambda e: e.activation(
            crow[0:1, 128:256], apv(ptf, 384, [[512, 1], [1, 128]]), AF.Exp),
            wait=[("pe", petr5)] + xfree.get("crow2", []),
            inc=[("act", 1)])["act"]
        q_cur = st_u.load(cumrow_d[c:c + 1, :], crow[0:1, 128:256],
                          wait=[("act", a_cum)])
        q_cum = st_m.load(cdm[:, :],
                          apv(cumrow_d, c * 128, [[0, 128], [1, 128]]),
                          wait=[q_cur] + xfree.get("cdm", []))
        d_cul = p.on("vector", lambda e: e.tensor_tensor(
            ccol, ccol, cs[:, 127:128], ALU.add),
            wait=[("pe", petr5)], inc=[("dve", 1)])["dve"]
        d_init = d_cul

        po_e3 = p.on("gpsimd", lambda e: e.tensor_tensor(
            e3s[:, :], expT, cdm[:, :], ALU.mult),
            wait=[q_cum, ("act", a_csT)] + xfree.get("e3s", []),
            inc=[("pool", 1)])["pool"]
        q_e3 = st_f.load(e3_sp[c], e3s[:, :], wait=[("pool", po_e3)])
        q_ex = p.dma("sync", expcs_sp[c], expT, "qs", wait=[("act", a_csT)])

        po_xdt = p.on("gpsimd", lambda e: e.tensor_tensor(
            xdt[:, :], xtc[:, :],
            apv(dtT, 0, [[128, 128], [1, 128], [0, 64]]), ALU.mult),
            wait=[qx, ("act", a_dtT)] + xfree.get("xdt", []),
            inc=[("pool", 1)])["pool"]
        po_xdtd = p.on("gpsimd", lambda e: e.tensor_tensor(
            xdtd[:, :], xdt[:, :],
            apv(decT, 0, [[128, 128], [1, 128], [0, 64]]), ALU.mult),
            wait=[("act", a_decT)] + xfree.get("xdtd", []),
            inc=[("pool", 1)])["pool"]
        po_xD = p.on("gpsimd", lambda e: e.tensor_tensor(
            xD[:, :], xtc[:, :],
            apv(dsb, 0, [[128, 128], [1, 128], [0, 64]]), ALU.mult),
            wait=xfree.get("xD", []), inc=[("pool", 1)])["pool"]

        for g in range(8):
            gb = g % 2
            w_bc = [("act", gs[("aE", g - 2)])] if ("aE", g - 2) in gs else []
            q_bc = st_bl.load(bcsl[:, gb * 2048:(gb + 1) * 2048],
                              apv(csfl, c * 16384 + g * 2048,
                                  [[0, 128], [1, 2048]]),
                              wait=[q_csf] + w_bc)
            w_cb = [("dve", gs[("cbm", g - 1)])] if ("cbm", g - 1) in gs else []
            pe_cb = p.on("tensor", lambda e, g=g: e.matmul(
                pcb[:, :], bfm[:, g * 128:(g + 1) * 128],
                cfm[:, g * 128:(g + 1) * 128], start=True, stop=True),
                wait=[qb, qc] + w_cb, inc=[("pe", 1)])["pe"]
            d_cbm = p.on("vector", lambda e: e.tensor_tensor(
                cbm[:, :], pcb[:, :], trl[:, :], ALU.mult),
                wait=[("pe", pe_cb)], inc=[("dve", 1)])["dve"]
            gs[("cbm", g)] = d_cbm

            po_d = p.on("gpsimd", lambda e, g=g, gb=gb: e.tensor_tensor(
                bcsl[:, gb * 2048:(gb + 1) * 2048],
                bcsl[:, gb * 2048:(gb + 1) * 2048],
                apv(chk, 384 + g * 16, [[CROW, 128], [1, 16], [0, 128]]),
                ALU.subtract), wait=[q_bc, ("act", a_csT)],
                inc=[("pool", 1)])["pool"]
            po_m = p.on("gpsimd", lambda e, gb=gb: e.tensor_scalar(
                bcsl[:, gb * 2048:(gb + 1) * 2048],
                bcsl[:, gb * 2048:(gb + 1) * 2048], 0.0, None, ALU.min),
                wait=[("pool", po_d)], inc=[("pool", 1)])["pool"]
            gs[("pod", g)] = po_m
            w_esb = [("dve", gs[("sc", g - 2)])] if ("sc", g - 2) in gs else []
            a_E = p.on("scalar", lambda e, gb=gb: e.activation(
                e_sb[:, gb * 2048:(gb + 1) * 2048],
                bcsl[:, gb * 2048:(gb + 1) * 2048], AF.Exp),
                wait=[("pool", po_m)] + w_esb, inc=[("act", 1)])["act"]
            gs[("aE", g)] = a_E
            w_sc = [("pe", gs[("ydiag", g - 2)])] if ("ydiag", g - 2) in gs else []
            d_sc = p.on("vector", lambda e, g=g, gb=gb: e.tensor_tensor(
                sc_sb[:, gb * 2048:(gb + 1) * 2048],
                e_sb[:, gb * 2048:(gb + 1) * 2048],
                apv(cbm, 0, [[128, 128], [0, 16], [1, 128]]), ALU.mult),
                wait=[("act", a_E), ("dve", d_cbm)] + w_sc,
                inc=[("dve", 1)])["dve"]
            gs[("sc", g)] = d_sc

            w_btm = [("pe", gs[("states", g - 1)])] \
                if ("states", g - 1) in gs else []
            pe_bt = p.on("tensor", lambda e, g=g: e.transpose(
                ptb[:, :], bfm[:, g * 128:(g + 1) * 128], idb[:, :]),
                wait=[qb] + w_btm +
                ([("act", gs[("btm_ev", g - 1)])] if ("btm_ev", g - 1) in gs else []),
                inc=[("pe", 1)])["pe"]
            a_bt = p.on("scalar", lambda e: e.copy(btm[:, :], ptb[:, :]),
                        wait=[("pe", pe_bt)], inc=[("act", 1)])["act"]
            gs[("btm_ev", g)] = a_bt

            w_y = [("dve", gs[("ybf", g - 1)])] if ("ybf", g - 1) in gs else []

            def f_ydiag(e, g=g, gb=gb):
                last = None
                for h in range(16):
                    last = e.matmul(pY[:, h * 64:(h + 1) * 64],
                                    sc_sb[:, gb * 2048 + h * 128:
                                          gb * 2048 + (h + 1) * 128],
                                    xdt[:, (g * 16 + h) * 64:(g * 16 + h + 1) * 64],
                                    start=(h % 8 == 0), stop=False)
                for k in range(2):
                    last = e.matmul(pY[:, k * 512:(k + 1) * 512], idb[:, :],
                                    xD[:, g * 1024 + k * 512: g * 1024 + (k + 1) * 512],
                                    start=False, stop=True)
                return last
            pe_yd = p.on("tensor", f_ydiag,
                         wait=[("dve", d_sc), ("pool", po_xD)] + w_y,
                         inc=[("pe", 1)])["pe"]
            gs[("ydiag", g)] = pe_yd

            cb_w = [("act", cbf_prev[g])] if g in cbf_prev \
                else [("dve", setup_dve)]
            w_px = [("dve", gs[("carry", g - 1)])] if ("carry", g - 1) in gs else []

            def f_yoff(e, g=g):
                last = None
                for k in range(2):
                    last = e.matmul(pX[:, k * 512:(k + 1) * 512],
                                    cfm[:, g * 128:(g + 1) * 128],
                                    carry_bf[:, g * 1024 + k * 512:
                                             g * 1024 + (k + 1) * 512],
                                    start=True, stop=True)
                return last
            pe_yo = p.on("tensor", f_yoff,
                         wait=[qc] + w_px + cb_w, inc=[("pe", 1)])["pe"]

            d_t1 = p.on("vector", lambda e, g=g: e.tensor_tensor(
                t1s[:, :], pX[:, :],
                apv(chk, 512 + g * 16, [[CROW, 128], [1, 16], [0, 64]]), ALU.mult),
                wait=[("pe", pe_yo), ("act", a_csT)], inc=[("dve", 1)])["dve"]
            gs[("t1", g)] = d_t1

            def f_st(e, g=g):
                last = None
                for k in range(2):
                    last = e.matmul(pX[:, k * 512:(k + 1) * 512], btm[:, :],
                                    xdtd[:, g * 1024 + k * 512:
                                         g * 1024 + (k + 1) * 512],
                                    start=True, stop=True)
                return last
            pe_st = p.on("tensor", f_st,
                         wait=[("act", a_bt), ("pool", po_xdtd), ("dve", d_t1)],
                         inc=[("pe", 1)])["pe"]
            gs[("states", g)] = pe_st
            d_ybf = p.on("vector", lambda e, g=g: e.tensor_tensor(
                ybf[:, g * 1024:(g + 1) * 1024], t1s[:, :], pY[:, :], ALU.add),
                wait=[("pe", pe_yd), ("dve", d_t1)] +
                ([prev_spill] if prev_spill is not None and g == 0 else []),
                inc=[("dve", 1)])["dve"]
            gs[("ybf", g)] = d_ybf

            w_t2 = [("dve", gs[("carry", g - 1)])] if ("carry", g - 1) in gs else []
            po_t2 = p.on("gpsimd", lambda e, g=g: e.tensor_tensor(
                t2s[:, :], carry[:, g * 1024:(g + 1) * 1024],
                apv(cdx, g * 16, [[128, 128], [1, 16], [0, 64]]), ALU.mult),
                wait=[q_cdx, ("pe", pe_yo)] + w_t2,
                inc=[("pool", 1)])["pool"]
            d_cy = p.on("vector", lambda e, g=g: e.tensor_tensor(
                carry[:, g * 1024:(g + 1) * 1024], t2s[:, :], pX[:, :], ALU.add),
                wait=[("pool", po_t2), ("pe", pe_st)], inc=[("dve", 1)])["dve"]
            gs[("carry", g)] = d_cy
            a_cbf = p.on("scalar", lambda e, g=g: e.copy(
                carry_bf[:, g * 1024:(g + 1) * 1024],
                carry[:, g * 1024:(g + 1) * 1024]),
                wait=[("dve", d_cy)], inc=[("act", 1)])["act"]
            cbf_prev[g] = a_cbf

        prev_spill = st_y.load(ypart_sp[c * 128:(c + 1) * 128, :], ybf[:, :],
                               wait=[("dve", gs[("ybf", 7)])])

        chk_free = [("act", a_cum), ("dve", max(gs[("t1", 7)], d_cul)),
                    ("pool", max(gs[("pod", 7)], po_e3)),
                    ("qs", q_ex), q_csf, ("pe", petr5)]
        xfree = {
            "x": [("pool", po_xD)],
            "bc": [("pe", gs[("states", 7)])],
            "dt": [("dve", d1), ("pe", petr2)],
            "dtT": [("pool", po_xdt)],
            "decT": [("pool", po_xdtd)],
            "crow": [q_cdr],
            "crow2": [q_cur],
            "cdx": [("pool", po_t2)],
            "cdm": [("pool", po_e3)],
            "e3s": [q_e3],
            "csf": [("pool", gs[("pod", 7)])],
            "xdt": [("pe", gs[("ydiag", 7)])],
            "xdtd": [("pe", gs[("states", 7)])],
            "xD": [("pe", gs[("ydiag", 7)])],
        }
        gs = {}

    q_s = p.dma("sync", slo_sp[:, :], carry_bf[:, :], "qs",
                wait=[("act", cbf_prev[7])])
    p.on("sync", None, wait=[(s, p.v(s)) for s in p.dma_sems if p.v(s) > 0])


def build_phase2_io(nc):
    io = {}
    io["xtm_sp"] = nc.dram_tensor("xtm_sp", [T, INTER], BF16, kind="ExternalInput").ap()
    io["bc_sp"] = nc.dram_tensor("bc_sp", [2048, T], BF16, kind="ExternalInput").ap()
    io["dt_sp"] = nc.dram_tensor("dt_sp", [128, T], F32, kind="ExternalInput").ap()
    io["drow_d"] = nc.dram_tensor("drow_d", [1, 128], F32, kind="ExternalInput").ap()
    io["tril_d"] = nc.dram_tensor("tril_d", [128, 128], F32, kind="ExternalInput").ap()
    io["idbf_d"] = nc.dram_tensor("idbf_d", [128, 128], BF16, kind="ExternalInput").ap()
    io["idf32_d"] = nc.dram_tensor("idf32_d", [128, 128], F32, kind="ExternalInput").ap()
    io["acol_d"] = nc.dram_tensor("acol_d", [128, 1], F32, kind="ExternalInput").ap()
    io["ypart_sp"] = nc.dram_tensor("ypart_sp", [T, INTER], BF16,
                                    kind="ExternalOutput").ap()
    io["slo_sp"] = nc.dram_tensor("slo_sp", [128, INTER], BF16,
                                  kind="ExternalOutput").ap()
    io["e3_sp"] = nc.dram_tensor("e3_sp", [NCH, 128, 128], F32,
                                 kind="ExternalOutput").ap()
    io["expcs_sp"] = nc.dram_tensor("expcs_sp", [NCH, 128, 128], F32,
                                    kind="ExternalOutput").ap()
    io["csfl"] = nc.dram_tensor("csfl", [NCH, 16384], F32, kind="Internal").ap()
    io["cdrow_d"] = nc.dram_tensor("cdrow_d", [NCH, 128], F32, kind="Internal").ap()
    io["cumrow_d"] = nc.dram_tensor("cumrow_d", [NCH, 128], F32, kind="Internal").ap()
    return io


def alloc_phase2_sb(nc, ctx):
    E = ctx.enter_context
    sb = {}
    sb["xtc"] = E(nc.sbuf_tensor("xtc", [128, INTER], BF16))
    sb["xdt"] = E(nc.sbuf_tensor("xdt", [128, INTER], BF16))
    sb["xdtd"] = E(nc.sbuf_tensor("xdtd", [128, INTER], BF16))
    sb["xD"] = E(nc.sbuf_tensor("xD", [128, INTER], BF16))
    sb["carry"] = E(nc.sbuf_tensor("carry", [128, INTER], F32))
    sb["carry_bf"] = E(nc.sbuf_tensor("carry_bf", [128, INTER], BF16))
    sb["ybf"] = E(nc.sbuf_tensor("ybf", [128, INTER], BF16))
    sb["bcsl"] = E(nc.sbuf_tensor("bcsl", [128, 4096], F32))
    sb["e_sb"] = E(nc.sbuf_tensor("e_sb", [128, 4096], BF16))
    sb["sc_sb"] = E(nc.sbuf_tensor("sc_sb", [128, 4096], BF16))
    sb["t1s"] = E(nc.sbuf_tensor("t1s", [128, 1024], F32))
    sb["t2s"] = E(nc.sbuf_tensor("t2s", [128, 1024], F32))
    sb["bfm"] = E(nc.sbuf_tensor("bfm", [128, 1024], BF16))
    sb["cfm"] = E(nc.sbuf_tensor("cfm", [128, 1024], BF16))
    sb["btm"] = E(nc.sbuf_tensor("btm", [128, 128], BF16))
    sb["chk"] = E(nc.sbuf_tensor("chk", [128, 1024], F32))
    sb["dtT"] = E(nc.sbuf_tensor("dtT", [128, 128], BF16))
    sb["decT"] = E(nc.sbuf_tensor("decT", [128, 128], BF16))
    sb["dsb"] = E(nc.sbuf_tensor("dsb", [128, 128], F32))
    sb["trl"] = E(nc.sbuf_tensor("trl", [128, 128], F32))
    sb["idb"] = E(nc.sbuf_tensor("idb", [128, 128], BF16))
    sb["idf"] = E(nc.sbuf_tensor("idf", [128, 128], F32))
    sb["acol"] = E(nc.sbuf_tensor("acol", [128, 1], F32))
    sb["cbm"] = E(nc.sbuf_tensor("cbm", [128, 128], F32))
    sb["zz"] = E(nc.sbuf_tensor("zz", [128, 128], F32))
    sb["cdx"] = E(nc.sbuf_tensor("cdx", [128, 128], F32))
    sb["cdm"] = E(nc.sbuf_tensor("cdm", [128, 128], F32))
    sb["e3s"] = E(nc.sbuf_tensor("e3s", [128, 128], F32))
    sb["crow"] = E(nc.sbuf_tensor("crow", [1, 256], F32))
    return sb


def alloc_phase2_ps(nc, ctx):
    E = ctx.enter_context
    ps = {}
    ps["pY"] = E(nc.psum_tensor("pY", [128, 1024], F32))
    ps["pX"] = E(nc.psum_tensor("pX", [128, 1024], F32))
    ps["pcb"] = E(nc.psum_tensor("pcb", [128, 128], F32))
    ps["ptf"] = E(nc.psum_tensor("ptf", [128, 512], F32))
    ps["ptb"] = E(nc.psum_tensor("ptb", [128, 128], BF16))
    return ps


def build_phase2():
    nc = bass.Bass("TRN2", target_bir_lowering=False)
    io = build_phase2_io(nc)
    with ExitStack() as ctx:
        sb = alloc_phase2_sb(nc, ctx)
        ps = alloc_phase2_ps(nc, ctx)
        p = Prog()
        p.sem("qs", "qg", "pe", "dve", "act", "pool", "bar")
        emit_phase2(p, io, sb, ps)
        sems = {n: ctx.enter_context(nc.semaphore(n)) for n in p.cnt}
        block = ctx.enter_context(nc.Block())
        p.install(block, sems)
    return nc


def host_layout_phase2(inputs, p1outs):
    D = np.asarray(inputs["L0_attn_D"], np.float32)
    drow = D.reshape(1, 128)
    tril = np.tril(np.ones((128, 128), np.float32))
    tril = np.triu(np.ones((128, 128), np.float32))
    idbf = np.eye(128, dtype=np.float32).astype(NPBF)
    idf32 = np.eye(128, dtype=np.float32)
    acol = -np.arange(1, 129, dtype=np.float32).reshape(128, 1)
    maps = []
    for core in range(NCORES):
        maps.append({"xtm_sp": p1outs[core]["xtm_sp"],
                     "bc_sp": p1outs[core]["bc_sp"],
                     "dt_sp": p1outs[core]["dt_sp"],
                     "drow_d": drow, "tril_d": tril, "idbf_d": idbf,
                     "idf32_d": idf32, "acol_d": acol})
    return maps



def emit_prestage(p, io, sb):
    wpre = sb["wpre"]
    nwsb = sb["nwsb"]
    q_nw = p.dma("sync", nwsb[:, :], io["normw_d"], "qs")
    st_ld = p.stream("pgl", "gpsimd", 2)
    st_st = p.stream("pst", "sync", 2)
    st_prev = {}
    for i in range(64):
        b = i % 2
        wv = wpre[:, b * 4096:(b + 1) * 4096]
        ww = [st_prev[i - 2]] if i - 2 in st_prev else []
        qi = st_ld.load(wv, io["WoutT"][i * 128:(i + 1) * 128, :], wait=ww)
        dv = p.on("vector", lambda e, wv=wv, i=i: e.tensor_scalar(
            wv, wv, nwsb[:, i:i + 1], None, ALU.mult),
            wait=[qi, ("qs", q_nw)], inc=[("dve", 1)])["dve"]
        st_prev[i] = st_st.load(io["wout_bf"][i * 128:(i + 1) * 128, :], wv,
                                wait=[("dve", dv)])
    jobs = []
    for cch in range(0, FH, 4096):
        w = min(4096, FH - cch)
        for r in range(8):
            jobs.append((io["w1qT"], io["w1_bf"], r * 128, cch, 128, w))
            jobs.append((io["w3qT"], io["w3_bf"], r * 128, cch, 128, w))
        jobs.append((io["b1T"], io["b1_bf"], 0, cch, 2, w))
        jobs.append((io["b3T"], io["b3_bf"], 0, cch, 2, w))
    for r in range(22):
        jobs.append((io["w2qT"], io["w2_bf"], r * 128, 0, 128, 4096))
    for r in range(32):
        jobs.append((io["a1T"], io["a1_bf"], r * 128, 0, 128, 2))
        jobs.append((io["a3T"], io["a3_bf"], r * 128, 0, 128, 2))
    for r in range(88):
        jobs.append((io["a2T"], io["a2_bf"], r * 128, 0, 128, 2))
    jobs.append((io["b2T"], io["b2_bf"], 0, 0, 2, 4096))
    ji = 64
    for (srcd, dstd, r0, c0, nr, ncol) in jobs:
        b = ji % 2
        wv = wpre[0:nr, b * 4096:b * 4096 + ncol]
        ww = [st_prev[ji - 2]] if ji - 2 in st_prev else []
        qi = st_ld.load(wv, srcd[r0:r0 + nr, c0:c0 + ncol], wait=ww)
        st_prev[ji] = st_st.load(dstd[r0:r0 + nr, c0:c0 + ncol], wv, wait=[qi])
        ji += 1


def emit_phase25(p, io, sb, ps):
    smv = sb["smv"]; ccf = sb["ccf"]; e3t = sb["e3t"]
    ytl = sb["ytl"]; ufc = sb["ufc"]; mcol = sb["mcol"]
    pQ = ps["pQ"]

    barp = p.quiesce()
    q_m = p.dma("sync", mcol[:, :], io["mask_d"], "qs", wait=[("bar", barp)])
    q_sin = p.dma("sync", smv[:, :], io["sin_d"], "qs")
    d_sm = p.on("vector", lambda e: e.tensor_scalar(
        smv[:, :], smv[:, :], mcol[:, 0:1], None, ALU.mult),
        wait=[("qs", q_sin)], inc=[("dve", 1)])["dve"]

    prev = {}
    pq_free = {0: None, 1: None}
    st_yl = p.stream("cyl", "sync", 2)
    st_ys = p.stream("cys", "sync", 2)
    st_cc = p.stream("ccc", "sync", 1)
    st_ce = p.stream("cce", "sync", 2)
    for c in range(NCH):
        yb = c % 2
        wy = [prev[("yst", c - 2)]] if ("yst", c - 2) in prev \
            else [("bar", barp)]
        q_y = st_yl.load(ytl[:, yb * 8192:(yb + 1) * 8192],
                         io["ypart_sp"][c * 128:(c + 1) * 128, :], wait=wy)
        wc = [("pe", prev[("mm", c - 1)])] if ("mm", c - 1) in prev else []
        q_c = st_cc.load(apv(ccf, 0, [[1024, 128], [128, 8], [1, 128]]),
                         apv(io["bc_sp"], 1024 * T + c * 128,
                             [[T, 128], [128 * T, 8], [1, 128]]), wait=wc)
        we = [("dve", prev[("e3", c - 2)])] if ("e3", c - 2) in prev else []
        q_e = st_ce.load(e3t[:, yb * 128:(yb + 1) * 128],
                         io["e3_sp"][c], wait=we)
        for g in range(8):
            pb = g % 2
            wp = [("dve", pq_free[pb])] if pq_free[pb] is not None else []

            def f_mm(e, g=g, pb=pb):
                last = None
                for k in range(2):
                    last = e.matmul(pQ[pb][:, k * 512:(k + 1) * 512],
                                    ccf[:, g * 128:(g + 1) * 128],
                                    smv[:, g * 1024 + k * 512:
                                        g * 1024 + (k + 1) * 512],
                                    start=True, stop=True)
                return last
            mm = p.on("tensor", f_mm,
                      wait=[q_c, ("dve", d_sm)] + wp,
                      inc=[("pe", 1)])["pe"]
            prev[("mm", c)] = mm
            wu = [("pool", prev[("y", c, g - 1)])] if ("y", c, g - 1) in prev \
                else ([("pool", prev[("ylast", c - 1)])]
                      if ("ylast", c - 1) in prev else [])
            dd = p.on("vector", lambda e, g=g, pb=pb, yb=yb: e.tensor_tensor(
                ufc[:, :], pQ[pb][:, :],
                apv(e3t, yb * 128 + g * 16, [[256, 128], [1, 16], [0, 64]]),
                ALU.mult),
                wait=[("pe", mm), q_e] + wu,
                inc=[("dve", 1)])["dve"]
            pq_free[pb] = dd
            prev[("e3", c)] = dd
            po = p.on("gpsimd", lambda e, g=g, yb=yb: e.tensor_tensor(
                ytl[:, yb * 8192 + g * 1024: yb * 8192 + (g + 1) * 1024],
                ytl[:, yb * 8192 + g * 1024: yb * 8192 + (g + 1) * 1024],
                ufc[:, :], ALU.add),
                wait=[("dve", dd), q_y], inc=[("pool", 1)])["pool"]
            prev[("y", c, g)] = po
        prev[("ylast", c)] = prev[("y", c, 7)]
        prev[("yst", c)] = st_ys.load(io["ycorr"][c * 128:(c + 1) * 128, :],
                                      ytl[:, yb * 8192:(yb + 1) * 8192],
                                      wait=[("pool", prev[("y", c, 7)])])
    return prev[("yst", NCH - 1)]


def emit_rowln(p, src, cols, sink, wait_pre, c0=16):
    c_sum, c_mean, c_sqa, c_sqb, c_var, c_m2 = (
        cols[:, c0 + i:c0 + i + 1] for i in range(6))
    c_rstd = cols[:, c0 + 4:c0 + 5]

    def f_sq(e):
        e.activation(sink[:, 0:2048], src[:, 0:2048], AF.Square, accum_out=c_sqa)
        return e.activation(sink[:, 2048:4096], src[:, 2048:4096], AF.Square,
                            accum_out=c_sqb)
    a_sq = p.on("scalar", f_sq, wait=wait_pre, inc=[("act", 1)])["act"]
    da = p.on("vector", lambda e: e.tensor_reduce(c_sum, src, AX.X, ALU.add),
              wait=wait_pre, inc=[("dve", 1)])["dve"]
    db = p.on("vector", lambda e: e.tensor_scalar_mul(c_mean, c_sum, 1.0 / DM),
              wait=[("dve", da)], inc=[("dve", 1)])["dve"]
    dc = p.on("vector", lambda e: e.tensor_tensor(c_m2, c_mean, c_mean, ALU.mult),
              wait=[("dve", db)], inc=[("dve", 1)])["dve"]
    dd = p.on("vector", lambda e: e.tensor_tensor(c_var, c_sqa, c_sqb, ALU.add),
              wait=[("act", a_sq), ("dve", dc)], inc=[("dve", 1)])["dve"]
    de = p.on("vector", lambda e: e.tensor_scalar(
        c_var, c_var, 1.0 / DM, EPS, ALU.mult, ALU.add),
        wait=[("dve", dd)], inc=[("dve", 1)])["dve"]
    df = p.on("vector", lambda e: e.tensor_tensor(c_var, c_var, c_m2, ALU.subtract),
              wait=[("dve", de)], inc=[("dve", 1)])["dve"]
    a_l = p.on("scalar", lambda e: e.activation(c_rstd, c_var, AF.Ln),
               wait=[("dve", df)], inc=[("act", 1)])["act"]
    a_r = p.on("scalar", lambda e: e.activation(c_rstd, c_rstd, AF.Exp, scale=-0.5),
               wait=[("act", a_l)], inc=[("act", 1)])["act"]
    return (c_mean, c_rstd, a_r, df)


PARANOID = False


def emit_phase3(p, io, sb, ps, q_corr, debug=False):
    def maybe_bar():
        if PARANOID:
            p.barrier_all()
    S1, S2, S3 = sb["S1"], sb["S2"], sb["S3"]
    xzf, c1f = sb["xzf"], sb["c1f"]
    wob = sb["wob"]; wf1 = sb["wf1"]; wf3 = sb["wf3"]
    b2s, b13 = sb["b2s"], sb["b13"]
    r1b = sb["r1b"]
    a1g, a3g, a2g = sb["a1g"], sb["a3g"], sb["a2g"]
    cols = sb["cols3"]; idb = sb["idb3"]
    pA, pB, pD, pC, pT = ps["pA"], ps["pB"], ps["pD"], ps["pC"], ps["pT"]
    S1R = rowstep(S1)
    po_tt = [pA[:, 0:512], pA[:, 512:1024], pB[:, 0:512], pB[:, 512:1024]]

    bar3a = p.quiesce()
    q_b = p.dma("sync", b2s[:, :], io["b2_bf"], "qs", wait=[("bar", bar3a)])
    p.dma("sync", apv(a1g, 0, [[64, 128], [2, 32], [1, 2]]),
          apv(io["a1_bf"], 0, [[2, 128], [256, 32], [1, 2]]), "qs")
    p.dma("sync", apv(a3g, 0, [[64, 128], [2, 32], [1, 2]]),
          apv(io["a3_bf"], 0, [[2, 128], [256, 32], [1, 2]]), "qs")
    p.dma("sync", apv(a2g, 0, [[176, 128], [2, 88], [1, 2]]),
          apv(io["a2_bf"], 0, [[2, 128], [256, 88], [1, 2]]), "qs")
    q_id = p.dma("sync", idb[:, :], io["idbf_d"], "qs")
    bar3 = p.quiesce()

    st_y3 = p.stream("y3l", "sync", 2)
    st_g3 = p.stream("g3l", "sync", 2)
    st_wo = p.stream("wo", "sync", 2)
    st_w1 = p.stream("wf1l", "sync", 2)
    st_w3 = p.stream("wf3l", "sync", 2)
    st_b1 = p.stream("b1l", "sync", 2)
    st_b3 = p.stream("b3l", "sync", 2)
    st_x3 = p.stream("x3l", "sync", 1)
    st_o3 = p.stream("o3s", "sync", 2)

    fr = {}
    for t4 in range(4):
        tau = t4 * 512
        last_ev = None
        for j in range(4):
            c = t4 * 4 + j
            ypt = S2[:, 0:8192]
            gpt = S2[:, 8192:16384]
            sg = S3[:, 0:8192]
            yg = S3[:, 8192:16384]
            q_y = st_y3.load(ypt, io["ycorr"][c * 128:(c + 1) * 128, :],
                             wait=[("bar", bar3)] + fr.get("S2a", []))
            q_g = st_g3.load(gpt, io["gate_sp"][c * 128:(c + 1) * 128, :])
            a_sg = p.on("scalar", lambda e, sg=sg, gpt=gpt: e.activation(
                sg, gpt, AF.Silu), wait=[q_g] + fr.get("S3a", []),
                inc=[("act", 1)])["act"]
            d_yg = p.on("vector", lambda e, yg=yg, ypt=ypt, sg=sg: e.tensor_tensor(
                yg, ypt, sg, ALU.mult),
                wait=[q_y, ("act", a_sg)] + fr.get("yga", []),
                inc=[("dve", 1)])["dve"]
            fr["S2a"] = [("dve", d_yg)]
            fr["S3a"] = [("dve", d_yg)]

            def f_sq(e, yg=yg):
                last = None
                for g in range(8):
                    last = e.activation(c1f[:, 0:1024],
                                        yg[:, g * 1024:(g + 1) * 1024], AF.Square,
                                        accum_out=cols[:, g:g + 1])
                return last
            a_ssq = p.on("scalar", f_sq,
                         wait=[("dve", d_yg)] + fr.get("c1f", []) +
                         fr.get("gncols", []),
                         inc=[("act", 1)])["act"]
            fr["c1f"] = [("act", a_ssq)]
            d_r1 = p.on("vector", lambda e: e.tensor_scalar(
                cols[:, 8:16], cols[:, 0:8], 1.0 / GROUP, EPS, ALU.mult, ALU.add),
                wait=[("act", a_ssq)], inc=[("dve", 1)])["dve"]
            fr["gncols"] = [("dve", d_r1)]
            a_r2 = p.on("scalar", lambda e: e.activation(
                cols[:, 8:16], cols[:, 8:16], AF.Ln),
                wait=[("dve", d_r1)], inc=[("act", 1)])["act"]
            a_r3 = p.on("scalar", lambda e: e.activation(
                cols[:, 8:16], cols[:, 8:16], AF.Exp, scale=-0.5),
                wait=[("act", a_r2)], inc=[("act", 1)])["act"]

            def f_ygn(e, yg=yg):
                last = None
                for g in range(8):
                    last = e.tensor_scalar(yg[:, g * 1024:(g + 1) * 1024],
                                           yg[:, g * 1024:(g + 1) * 1024],
                                           cols[:, 8 + g:9 + g], None, ALU.mult)
                return last
            d_ygn = p.on("vector", f_ygn, wait=[("act", a_r3)],
                         inc=[("dve", 1)])["dve"]
            pe_tr = None
            for bt in range(8):
                def f_tr(e, bt=bt, yg=yg):
                    last = None
                    for i in range(8):
                        it = bt * 8 + i
                        last = e.transpose(pT[:, i * 128:(i + 1) * 128],
                                           yg[:, it * 128:(it + 1) * 128],
                                           idb[:, :])
                    return last
                pe_tr = p.on("tensor", f_tr,
                             wait=[("dve", d_ygn), ("qs", q_id)] + fr.get("pT", []),
                             inc=[("pe", 1)])["pe"]

                def f_ev(e, bt=bt, j=j):
                    srcp = apv(pT, 0, [[1024, 128], [128, 8], [1, 128]])
                    dstp = apv(S1, (bt * 8) * 512 + j * 128,
                               [[S1R, 128], [512, 8], [1, 128]])
                    return e.copy(dstp, srcp)
                last_ev = p.on("scalar", f_ev,
                               wait=[("pe", pe_tr)] + fr.get("S1a", []),
                               inc=[("act", 1)])["act"]
                fr["pT"] = [("act", last_ev)]
            fr["yga"] = [("pe", pe_tr)]
        maybe_bar()
        ygn_done = last_ev
        if debug and t4 == 0:
            p.dma("sync", io["dbg_ygn"], S1[:, :], "qs", wait=[("act", ygn_done)])

        evB = {}
        mmB = {}
        for jn in range(8):
            for i in range(64):
                b = (jn * 64 + i) % 2
                u = jn * 64 + i
                wiw = [("pe", mmB[u - 2])] if u - 2 in mmB \
                    else fr.get("wob", [("bar", bar3)])
                q_w = st_wo.load(wob[:, b * 512:(b + 1) * 512],
                                 io["wout_bf"][i * 128:(i + 1) * 128,
                                               jn * 512:(jn + 1) * 512],
                                 wait=wiw)

                def f_mm(e, i=i, b=b):
                    last = None
                    for tt in range(4):
                        last = e.matmul(
                            po_tt[tt],
                            S1[:, i * 512 + tt * 128: i * 512 + tt * 128 + 128],
                            wob[:, b * 512:(b + 1) * 512],
                            start=(i == 0), stop=(i == 63))
                    return last
                wpsum = []
                if i == 0:
                    if jn >= 1:
                        wpsum = [("act", evB[(jn - 1, 3)])]
                    else:
                        wpsum = fr.get("pAB", [])
                mmB[u] = p.on("tensor", f_mm,
                              wait=[q_w, ("act", ygn_done)] + wpsum,
                              inc=[("pe", 1)])["pe"]
            for tt in range(4):
                obf = S2[:, tt * 4096:(tt + 1) * 4096]

                def f_ev(e, tt=tt, jn=jn, obf=obf):
                    return e.copy(obf[:, jn * 512:(jn + 1) * 512], po_tt[tt])
                evB[(jn, tt)] = p.on("scalar", f_ev,
                                     wait=[("pe", mmB[jn * 64 + 63])],
                                     inc=[("act", 1)])["act"]

        if debug and t4 == 0:
            p.dma("sync", io["dbg_obf"], S2[:, :], "qs",
                  wait=[("act", evB[(7, 3)])])
        maybe_bar()
        d_x1 = {}
        q_out_prev = fr.get("qout", None)
        for tt in range(4):
            obf = S2[:, tt * 4096:(tt + 1) * 4096]
            x1 = S1[:, tt * 4096:(tt + 1) * 4096]
            w_ev = [("act", evB[(7, tt)])] + fr.get("c1f", []) + \
                fr.get("lncols", [])
            q_x0 = st_x3.load(
                xzf[:, :],
                io["x0c"][3 + tau + tt * 128: 3 + tau + (tt + 1) * 128, :],
                wait=fr.get("xzf", []))
            ln = emit_rowln(p, obf, cols, c1f, w_ev)
            fr["lncols"] = [("dve", ln[3]), ("act", ln[2])]
            d_c1 = p.on("vector", lambda e, obf=obf, m=ln[0], r=ln[1]:
                        e.tensor_scalar(c1f[:, :], obf, m, r,
                                        ALU.subtract, ALU.mult),
                        wait=[("act", ln[2])], inc=[("dve", 1)])["dve"]
            d_x1[tt] = p.on("vector", lambda e, x1=x1: e.tensor_tensor(
                x1, c1f[:, :], xzf[:, :], ALU.add),
                wait=[("dve", d_c1), q_x0], inc=[("dve", 1)])["dve"]
            fr["xzf"] = [("dve", d_x1[tt])]
            fr["c1f"] = [("dve", d_x1[tt])]

        maybe_bar()
        h2f_off = 16384
        tr_last = None
        pe_tr = None
        for tt in range(4):
            x1 = S1[:, tt * 4096:(tt + 1) * 4096]
            ln = emit_rowln(p, x1, cols, c1f,
                            [("dve", d_x1[tt])] + fr.get("c1f", []) +
                            fr.get("lncols", []))
            fr["lncols"] = [("dve", ln[3]), ("act", ln[2])]
            d_h2 = p.on("vector", lambda e, x1=x1, m=ln[0], r=ln[1]:
                        e.tensor_scalar(S2[:, 0:4096], x1, m, r,
                                        ALU.subtract, ALU.mult),
                        wait=[("act", ln[2])] + fr.get("h2t", []),
                        inc=[("dve", 1)])["dve"]
            fr["c1f"] = [("act", ln[2])]
            for bt in range(4):
                def f_tr(e, bt=bt):
                    last = None
                    for i in range(8):
                        it = bt * 8 + i
                        last = e.transpose(pT[:, i * 128:(i + 1) * 128],
                                           S2[:, it * 128:(it + 1) * 128],
                                           idb[:, :])
                    return last
                pe_tr = p.on("tensor", f_tr,
                             wait=[("dve", d_h2)] + fr.get("pT", []),
                             inc=[("pe", 1)])["pe"]

                def f_ev(e, bt=bt, tt=tt):
                    srcp = apv(pT, 0, [[1024, 128], [128, 8], [1, 128]])
                    dstp = apv(S1, h2f_off + (bt * 8) * 512 + tt * 128,
                               [[S1R, 128], [512, 8], [1, 128]])
                    return e.copy(dstp, srcp)
                tr_last = p.on("scalar", f_ev, wait=[("pe", pe_tr)],
                               inc=[("act", 1)])["act"]
                fr["pT"] = [("act", tr_last)]
            fr["h2t"] = [("pe", pe_tr)]
        if debug and t4 == 0:
            p.dma("sync", io["dbg_x1"], S1[:, :], "qs", wait=[("act", tr_last)])
        d_xs = None
        for it in range(8):
            def hv(q, it=it):
                o = h2f_off + (q * 8 + it) * 512
                return S1[:, o:o + 512]
            p.on("vector", lambda e, hv=hv: e.tensor_tensor(
                c1f[:, 0:512], hv(0), hv(1), ALU.add),
                wait=[("act", tr_last)] + fr.get("S3d", []) + fr.get("c1f", []))
            p.on("vector", lambda e, hv=hv: e.tensor_tensor(
                c1f[:, 512:1024], hv(2), hv(3), ALU.add))
            d_xs = p.on("vector", lambda e, it=it: e.tensor_tensor(
                S3[:, it * 512:(it + 1) * 512], c1f[:, 0:512], c1f[:, 512:1024],
                ALU.add), inc=[("dve", 1)])["dve"]
        fr["c1f"] = [("dve", d_xs)]

        maybe_bar()
        def f_r1(e):
            last = None
            for kt in range(32):
                last = e.matmul(pD[0:2, 0:512], a1g[:, kt * 2:(kt + 1) * 2],
                                S1[:, h2f_off + kt * 512:h2f_off + (kt + 1) * 512],
                                start=(kt == 0), stop=(kt == 31))
            return last
        pe_r1 = p.on("tensor", f_r1,
                     wait=[("act", tr_last), ("qs", q_id)] + fr.get("pD", []),
                     inc=[("pe", 1)])["pe"]

        def f_r3(e):
            last = None
            for kt in range(32):
                last = e.matmul(pD[0:2, 512:1024], a3g[:, kt * 2:(kt + 1) * 2],
                                S1[:, h2f_off + kt * 512:h2f_off + (kt + 1) * 512],
                                start=(kt == 0), stop=(kt == 31))
            return last
        pe_r3 = p.on("tensor", f_r3, inc=[("pe", 1)])["pe"]
        a_rr = p.on("scalar", lambda e: (
            e.activation(r1b[0:2, 0:512], pD[0:2, 0:512], AF.Copy, scale=0.5),
            e.activation(r1b[0:2, 512:1024], pD[0:2, 512:1024], AF.Copy,
                         scale=0.5))[-1],
            wait=[("pe", pe_r3)] + fr.get("r1b", []), inc=[("act", 1)])["act"]

        g1t = S3[:, 15360:15872]
        g3t = S3[:, 15872:16384]
        evD = {}
        mmD = {}
        r2mm = None
        prev_ag3 = None
        for q in range(4):
            for jh in range(22):
                u = q * 22 + jh
                b = u % 2
                ww = [("pe", mmD[u - 2])] if u - 2 in mmD \
                    else fr.get("wfb", [])
                q_w1 = st_w1.load(apv(wf1, b * 1024,
                                      [[2048, 128], [128, 8], [1, 128]]),
                                  apv(io["w1_bf"], q * FQ + jh * 128,
                                      [[FH, 128], [128 * FH, 8], [1, 128]]),
                                  wait=ww)
                q_w3 = st_w3.load(apv(wf3, b * 1024,
                                      [[2048, 128], [128, 8], [1, 128]]),
                                  apv(io["w3_bf"], q * FQ + jh * 128,
                                      [[FH, 128], [128 * FH, 8], [1, 128]]))
                wps = [("act", prev_ag3)] if prev_ag3 is not None else []
                jc = q * FQ + jh * 128
                qb1 = st_b1.load(b13[0:2, b * 256:b * 256 + 128],
                                 io["b1_bf"][0:2, jc:jc + 128], wait=ww)
                qb3 = st_b3.load(b13[0:2, b * 256 + 128:b * 256 + 256],
                                 io["b3_bf"][0:2, jc:jc + 128])

                def f_g(e, b=b, jc=jc):
                    last = None
                    for kt in range(8):
                        e.matmul(pD[:, 0:512],
                                 wf1[:, b * 1024 + kt * 128: b * 1024 + (kt + 1) * 128],
                                 S3[:, kt * 512:(kt + 1) * 512],
                                 start=(kt == 0), stop=False)
                    e.matmul(pD[:, 0:512], b13[0:2, b * 256:b * 256 + 128],
                             r1b[0:2, 0:512], start=False, stop=True)
                    for kt in range(8):
                        e.matmul(pD[:, 512:1024],
                                 wf3[:, b * 1024 + kt * 128: b * 1024 + (kt + 1) * 128],
                                 S3[:, kt * 512:(kt + 1) * 512],
                                 start=(kt == 0), stop=False)
                    last = e.matmul(pD[:, 512:1024],
                                    b13[0:2, b * 256 + 128:b * 256 + 256],
                                    r1b[0:2, 512:1024], start=False, stop=True)
                    return last
                wfirst = [("act", a_rr), ("dve", d_xs)] if u == 0 else []
                mmD[u] = p.on("tensor", f_g,
                              wait=[q_w1, q_w3, qb1, qb3] + wps + wfirst,
                              inc=[("pe", 1)])["pe"]
                wgu = [("dve", evD[u - 1])] if u - 1 in evD else []
                a_g1 = p.on("scalar", lambda e: e.activation(
                    g1t, pD[:, 0:512], AF.Silu),
                    wait=[("pe", mmD[u])] + fr.get("g1t", []) + wgu,
                    inc=[("act", 1)])["act"]
                a_g3 = p.on("scalar", lambda e: e.activation(
                    g3t, pD[:, 512:1024], AF.Copy),
                    inc=[("act", 1)])["act"]
                prev_ag3 = a_g3
                d_gu = p.on("vector", lambda e: e.tensor_tensor(
                    g1t, g1t, g3t, ALU.mult),
                    wait=[("act", a_g3)], inc=[("dve", 1)])["dve"]
                fr["g1t"] = [("dve", 0)]
                ys_sl = S3[:, 4096 + jh * 512: 4096 + (jh + 1) * 512]
                if q == 0:
                    po_ys = p.on("gpsimd", lambda e, ys_sl=ys_sl: e.tensor_copy(
                        ys_sl, g1t), wait=[("dve", d_gu)] + fr.get("ysd", []),
                        inc=[("pool", 1)])["pool"]
                else:
                    po_ys = p.on("gpsimd", lambda e, ys_sl=ys_sl: e.tensor_tensor(
                        ys_sl, ys_sl, g1t, ALU.add),
                        wait=[("dve", d_gu)], inc=[("pool", 1)])["pool"]
                r2mm = p.on("tensor", lambda e, u=u: e.matmul(
                    pC[0:2, 0:512], a2g[:, u * 2:(u + 1) * 2], g1t,
                    start=(u == 0), stop=(u == 87)),
                    wait=[("dve", d_gu)] +
                    (fr.get("pC", []) if u == 0 else []),
                    inc=[("pe", 1)])["pe"]
                evD[u] = d_gu
                fr["g1t"] = [("pool", po_ys), ("pe", r2mm)]
        if debug and t4 == 0:
            p.dma("sync", io["dbg_ys"], S3[:, :], "qs",
                  wait=[("pool", po_ys), ("dve", d_xs)])
        a_r2b = p.on("scalar", lambda e: e.activation(
            r1b[0:2, 1024:1536], pC[0:2, 0:512], AF.Copy, scale=0.5),
            wait=[("pe", r2mm)], inc=[("act", 1)])["act"]
        fr["pC"] = [("act", a_r2b)]
        fr["r1b"] = [("pe", r2mm)]

        maybe_bar()
        evE = {}
        mmE = {}
        for jn in range(8):
            for h1t in range(22):
                u = jn * 22 + h1t
                b = u % 2
                ww = [("pe", mmE[("u", u - 2)])] if ("u", u - 2) in mmE \
                    else [("pe", mmB[7 * 64 + 63])]
                q_w = st_wo.load(wob[:, b * 512:(b + 1) * 512],
                                 io["w2_bf"][h1t * 128:(h1t + 1) * 128,
                                             jn * 512:(jn + 1) * 512],
                                 wait=ww + ([("pe", mmB[7 * 64 + 63])]
                                            if jn == 0 and h1t == 0 else []))

                def f_mm(e, h1t=h1t, b=b):
                    last = None
                    for tt in range(4):
                        last = e.matmul(
                            po_tt[tt],
                            S3[:, 4096 + h1t * 512 + tt * 128:
                               4096 + h1t * 512 + tt * 128 + 128],
                            wob[:, b * 512:(b + 1) * 512],
                            start=(h1t == 0), stop=False)
                    return last
                wps = []
                if h1t == 0:
                    wps = [("act", evE[(jn - 1, 3)])] if jn >= 1 \
                        else [("act", evB[(7, 3)]), ("pool", po_ys)]
                mmE[(jn, h1t)] = p.on("tensor", f_mm,
                                      wait=[q_w] + wps,
                                      inc=[("pe", 1)])["pe"]
                mmE[("u", u)] = mmE[(jn, h1t)]

            def f_lora(e, jn=jn):
                last = None
                for tt in range(4):
                    last = e.matmul(po_tt[tt], r1b[0:2, 1024 + tt * 128:
                                                   1024 + (tt + 1) * 128],
                                    b2s[0:2, jn * 512:(jn + 1) * 512],
                                    start=False, stop=True)
                return last
            mmE[(jn, 21)] = p.on("tensor", f_lora, wait=[("act", a_r2b)],
                                 inc=[("pe", 1)])["pe"]
            for tt in range(4):
                o2 = S2[:, tt * 4096:(tt + 1) * 4096]

                def f_ev(e, tt=tt, jn=jn, o2=o2):
                    return e.copy(o2[:, jn * 512:(jn + 1) * 512], po_tt[tt])
                evE[(jn, tt)] = p.on("scalar", f_ev,
                                     wait=[("pe", mmE[(jn, 21)])] +
                                     (fr.get("S2e", []) if jn == 0 else []),
                                     inc=[("act", 1)])["act"]

        maybe_bar()
        q_outs = []
        for tt in range(4):
            o2 = S2[:, tt * 4096:(tt + 1) * 4096]
            x1 = S1[:, tt * 4096:(tt + 1) * 4096]
            ln = emit_rowln(p, o2, cols, c1f,
                            [("act", evE[(7, tt)])] + fr.get("c1f", []) +
                            fr.get("lncols", []))
            fr["lncols"] = [("dve", ln[3]), ("act", ln[2])]
            d_c1 = p.on("vector", lambda e, o2=o2, m=ln[0], r=ln[1]:
                        e.tensor_scalar(c1f[:, :], o2, m, r,
                                        ALU.subtract, ALU.mult),
                        wait=[("act", ln[2])], inc=[("dve", 1)])["dve"]
            d_o = p.on("vector", lambda e, x1=x1: e.tensor_tensor(
                c1f[:, :], c1f[:, :], x1, ALU.add),
                wait=[("dve", d_c1)], inc=[("dve", 1)])["dve"]
            q_o = st_o3.load(io["out_d"][tau + tt * 128: tau + (tt + 1) * 128, :],
                             c1f[:, :], wait=[("dve", d_o)])
            q_outs.append(q_o)
            fr["c1f"] = [q_o]

        maybe_bar()
        fr["wob"] = [("pe", mmE[(7, 21)])]
        fr["wfb"] = [("pe", mmE[(7, 21)])]
        fr["S2a"] = [("dve", p.v("dve"))]
        fr["S3a"] = [("pe", mmE[(7, 21)])]
        fr["S1a"] = [q_outs[-1], ("pe", mmD[87])]
        fr["pAB"] = [("act", evE[(7, 3)])]
        fr["pD"] = [("pe", mmD[87]), ("act", a_g3)]
        fr["ysd"] = [("pe", mmE[(7, 21)])]
        fr["S3d"] = [("pe", mmE[(7, 21)])]
        fr["S2e"] = []
        fr["qout"] = None

    p.on("sync", None, wait=[(s, p.v(s)) for s in p.dma_sems if p.v(s) > 0])


def build_phase3_io(nc):
    io = {}
    EI = "ExternalInput"
    io["x0c"] = nc.dram_tensor("x0c", [TH, DM], F32, kind=EI).ap()
    io["ypart_sp"] = nc.dram_tensor("ypart_sp", [T, INTER], BF16, kind=EI).ap()
    io["gate_sp"] = nc.dram_tensor("gate_sp", [T, INTER], BF16, kind=EI).ap()
    io["bc_sp"] = nc.dram_tensor("bc_sp", [2048, T], BF16, kind=EI).ap()
    io["e3_sp"] = nc.dram_tensor("e3_sp", [NCH, 128, 128], F32, kind=EI).ap()
    io["sin_d"] = nc.dram_tensor("sin_d", [128, INTER], BF16, kind=EI).ap()
    io["mask_d"] = nc.dram_tensor("mask_d", [128, 1], F32, kind=EI).ap()
    io["idbf_d"] = nc.dram_tensor("idbf_d", [128, 128], BF16, kind=EI).ap()
    io["normw_d"] = nc.dram_tensor("normw_d", [128, 64], F32, kind=EI).ap()
    io["WoutT"] = nc.dram_tensor("WoutT", [INTER, DM], F32, kind=EI).ap()
    io["w1qT"] = nc.dram_tensor("w1qT", [1024, FH], F32, kind=EI).ap()
    io["w3qT"] = nc.dram_tensor("w3qT", [1024, FH], F32, kind=EI).ap()
    io["w2qT"] = nc.dram_tensor("w2qT", [FQ, DM], F32, kind=EI).ap()
    io["a1T"] = nc.dram_tensor("a1T", [DM, 2], F32, kind=EI).ap()
    io["a3T"] = nc.dram_tensor("a3T", [DM, 2], F32, kind=EI).ap()
    io["a2T"] = nc.dram_tensor("a2T", [FH, 2], F32, kind=EI).ap()
    io["b1T"] = nc.dram_tensor("b1T", [2, FH], F32, kind=EI).ap()
    io["b3T"] = nc.dram_tensor("b3T", [2, FH], F32, kind=EI).ap()
    io["b2T"] = nc.dram_tensor("b2T", [2, DM], F32, kind=EI).ap()
    io["out_d"] = nc.dram_tensor("out_d", [T, DM], F32, kind="ExternalOutput").ap()
    for nm, shp, dt in (("wout_bf", [INTER, DM], BF16),
                        ("w1_bf", [1024, FH], BF16), ("w3_bf", [1024, FH], BF16),
                        ("w2_bf", [FQ, DM], BF16),
                        ("a1_bf", [DM, 2], BF16), ("a3_bf", [DM, 2], BF16),
                        ("a2_bf", [FH, 2], BF16),
                        ("b1_bf", [2, FH], BF16), ("b3_bf", [2, FH], BF16),
                        ("b2_bf", [2, DM], BF16),
                        ("ycorr", [T, INTER], BF16)):
        io[nm] = nc.dram_tensor(nm, shp, dt, kind="Internal").ap()
    return io


def alloc_prestage_sb(nc, ctx):
    E = ctx.enter_context
    return {"wpre": E(nc.sbuf_tensor("wpre", [128, 8192], BF16)),
            "nwsb": E(nc.sbuf_tensor("nwsb", [128, 64], F32))}


def alloc_phase25_sb(nc, ctx):
    E = ctx.enter_context
    return {"smv": E(nc.sbuf_tensor("smv", [128, INTER], BF16)),
            "ccf": E(nc.sbuf_tensor("ccf", [128, 1024], BF16)),
            "e3t": E(nc.sbuf_tensor("e3t", [128, 256], F32)),
            "ytl": E(nc.sbuf_tensor("ytl", [128, 2 * INTER], BF16)),
            "ufc": E(nc.sbuf_tensor("ufc", [128, 1024], F32)),
            "mcol": E(nc.sbuf_tensor("mcol", [128, 1], F32))}


def alloc_phase3_sb(nc, ctx):
    E = ctx.enter_context
    sb = {}
    sb["S1"] = E(nc.sbuf_tensor("S1", [128, 32768], BF16))
    sb["S2"] = E(nc.sbuf_tensor("S2", [128, 16384], BF16))
    sb["S3"] = E(nc.sbuf_tensor("S3", [128, 16384], BF16))
    sb["xzf"] = E(nc.sbuf_tensor("xzf", [128, 4096], F32))
    sb["c1f"] = E(nc.sbuf_tensor("c1f", [128, 4096], F32))
    sb["wob"] = E(nc.sbuf_tensor("wob", [128, 1024], BF16))
    sb["wf1"] = E(nc.sbuf_tensor("wf1", [128, 2048], BF16))
    sb["wf3"] = E(nc.sbuf_tensor("wf3", [128, 2048], BF16))
    sb["b2s"] = E(nc.sbuf_tensor("b2s", [2, DM], BF16))
    sb["b13"] = E(nc.sbuf_tensor("b13", [2, 512], BF16))
    sb["r1b"] = E(nc.sbuf_tensor("r1b", [2, 1536], BF16))
    sb["a1g"] = E(nc.sbuf_tensor("a1g", [128, 64], BF16))
    sb["a3g"] = E(nc.sbuf_tensor("a3g", [128, 64], BF16))
    sb["a2g"] = E(nc.sbuf_tensor("a2g", [128, 176], BF16))
    sb["cols3"] = E(nc.sbuf_tensor("cols3", [128, 24], F32))
    sb["idb3"] = E(nc.sbuf_tensor("idb3", [128, 128], BF16))
    return sb


def alloc_phase3_ps(nc, ctx):
    E = ctx.enter_context
    ps = {}
    ps["pA"] = E(nc.psum_tensor("pA", [128, 1024], F32))
    ps["pB"] = E(nc.psum_tensor("pB", [128, 1024], F32))
    ps["pD"] = E(nc.psum_tensor("pD", [128, 1024], F32))
    ps["pC"] = E(nc.psum_tensor("pC", [128, 512], F32))
    ps["pT"] = E(nc.psum_tensor("pT", [128, 1024], BF16))
    ps["pQ"] = [ps["pA"], ps["pB"]]
    return ps


def build_phase3(debug=False):
    nc = bass.Bass("TRN2", target_bir_lowering=False)
    io = build_phase3_io(nc)
    if debug:
        io["ycorr"] = nc.dram_tensor("ycorr_o", [T, INTER], BF16,
                                     kind="ExternalOutput").ap()
        io["dbg_ygn"] = nc.dram_tensor("dbg_ygn", [128, 32768], BF16,
                                       kind="ExternalOutput").ap()
        io["dbg_obf"] = nc.dram_tensor("dbg_obf", [128, 16384], BF16,
                                       kind="ExternalOutput").ap()
        io["dbg_x1"] = nc.dram_tensor("dbg_x1", [128, 32768], BF16,
                                      kind="ExternalOutput").ap()
        io["dbg_ys"] = nc.dram_tensor("dbg_ys", [128, 16384], BF16,
                                      kind="ExternalOutput").ap()
    with ExitStack() as ctx:
        p = Prog()
        p.sem("qs", "qg", "pe", "dve", "act", "pool", "bar")
        ps = alloc_phase3_ps(nc, ctx)
        with ExitStack() as c0:
            sbp = alloc_prestage_sb(nc, c0)
            emit_prestage(p, io, sbp)
        with ExitStack() as c1:
            sb25 = alloc_phase25_sb(nc, c1)
            q_corr = emit_phase25(p, io, sb25, ps)
        sb = alloc_phase3_sb(nc, ctx)
        emit_phase3(p, io, sb, ps, q_corr, debug=debug)
        sems = {n: ctx.enter_context(nc.semaphore(n)) for n in p.cnt}
        block = ctx.enter_context(nc.Block())
        p.install(block, sems)
    return nc


def host_layout_phase3(inputs, p1outs, p2outs):
    x0 = np.asarray(inputs["L0_attn_hidden_states"], np.float32)

    def padh(w):
        w = np.asarray(w, np.float32)
        q = w.reshape(4, 2752, -1)
        out = np.zeros((4, FQ, w.shape[1]), np.float32)
        out[:, :2752] = q
        return out.reshape(FH, w.shape[1])

    WoutT = np.ascontiguousarray(np.asarray(inputs["L0_attn_out_proj_w"],
                                            np.float32).T)
    normw = np.ascontiguousarray(np.asarray(inputs["L0_attn_norm_w"], np.float32)
                                 .reshape(64, 128).T)
    w1qT = np.ascontiguousarray(padh(inputs["L0_ffn_w1q"]).T)
    w3qT = np.ascontiguousarray(padh(inputs["L0_ffn_w3q"]).T)
    w2qT = np.zeros((FQ, DM), np.float32)
    w2qT[:2752] = np.asarray(inputs["L0_ffn_w2q"], np.float32).T
    a1T = np.ascontiguousarray(np.asarray(inputs["L0_ffn_a1"], np.float32).T)
    a3T = np.ascontiguousarray(np.asarray(inputs["L0_ffn_a3"], np.float32).T)
    a2T = np.ascontiguousarray(padh(np.ascontiguousarray(
        np.asarray(inputs["L0_ffn_a2"], np.float32).T)))
    b1T = np.ascontiguousarray(padh(inputs["L0_ffn_b1"]).T)
    b3T = np.ascontiguousarray(padh(inputs["L0_ffn_b3"]).T)
    b2T = np.ascontiguousarray(np.asarray(inputs["L0_ffn_b2"], np.float32).T)
    idbf = np.eye(128, dtype=np.float32).astype(NPBF)
    maps = []
    for core in range(NCORES):
        b, s = core // 2, core % 2
        shard = x0[b, s * T:(s + 1) * T]
        halo = np.zeros((3, DM), np.float32) if s == 0 else x0[b, s * T - 3: s * T]
        x0c = np.ascontiguousarray(np.concatenate([halo, shard], 0))
        if s == 1:
            sin = np.asarray(p2outs[core - 1]["slo_sp"])
            mask = np.full((128, 1), 1.0, np.float32)
        else:
            sin = np.zeros((128, INTER), NPBF)
            mask = np.zeros((128, 1), np.float32)
        maps.append({
            "x0c": x0c, "ypart_sp": p2outs[core]["ypart_sp"],
            "gate_sp": p1outs[core]["gate_sp"], "bc_sp": p1outs[core]["bc_sp"],
            "e3_sp": np.asarray(p2outs[core]["e3_sp"], np.float32),
            "sin_d": sin, "mask_d": mask, "idbf_d": idbf, "normw_d": normw,
            "WoutT": WoutT, "w1qT": w1qT, "w3qT": w3qT, "w2qT": w2qT,
            "a1T": a1T, "a3T": a3T, "a2T": a2T, "b1T": b1T, "b3T": b3T,
            "b2T": b2T})
    return maps



def _padh(w):
    w = np.asarray(w)
    q = w.reshape(4, 2752, -1)
    out = np.zeros((4, FQ, w.shape[1]), w.dtype)
    out[:, :2752] = q
    return out.reshape(FH, w.shape[1])


def host_layout_phase1(inputs):
    x0 = np.asarray(inputs["L0_attn_hidden_states"], np.float32)
    WinT = np.ascontiguousarray(np.asarray(inputs["L0_attn_in_proj_w"], np.float32).T)
    cw = np.asarray(inputs["L0_attn_conv_w"], np.float32).reshape(CONV, 4)
    cw_d = np.ascontiguousarray(cw.reshape(80, 128, 4).transpose(1, 0, 2).reshape(128, 320))
    cb_d = np.ascontiguousarray(np.asarray(inputs["L0_attn_conv_b"], np.float32)
                                .reshape(80, 128).T)
    dtb_d = np.asarray(inputs["L0_attn_dt_bias"], np.float32).reshape(128, 1)
    idbf_d = np.eye(128, dtype=np.float32).astype(NPBF)
    maps = []
    for core in range(NCORES):
        b, s = core // 2, core % 2
        shard = x0[b, s * T:(s + 1) * T]
        halo = np.zeros((3, DM), np.float32) if s == 0 else x0[b, s * T - 3: s * T]
        x0c = np.ascontiguousarray(np.concatenate([halo, shard], 0))
        maps.append({"x0c": x0c, "WinT": WinT, "cw_d": cw_d, "cb_d": cb_d,
                     "dtb_d": dtb_d, "idbf_d": idbf_d})
    return maps


_NC_CACHE = {}


def _get_nc(name, builder):
    if name not in _NC_CACHE:
        _NC_CACHE[name] = builder()
    return _NC_CACHE[name]






def build_fused():
    nc = bass.Bass("TRN2", target_bir_lowering=False)
    EI = "ExternalInput"
    io = {}
    io["x0c"] = nc.dram_tensor("x0c", [TH, DM], F32, kind=EI).ap()
    io["WinT"] = nc.dram_tensor("WinT", [DM, PROJ], F32, kind=EI).ap()
    io["cw_d"] = nc.dram_tensor("cw_d", [128, 320], F32, kind=EI).ap()
    io["cb_d"] = nc.dram_tensor("cb_d", [128, 80], F32, kind=EI).ap()
    io["dtb_d"] = nc.dram_tensor("dtb_d", [128, 1], F32, kind=EI).ap()
    io["idbf_d"] = nc.dram_tensor("idbf_d", [128, 128], BF16, kind=EI).ap()
    io["drow_d"] = nc.dram_tensor("drow_d", [1, 128], F32, kind=EI).ap()
    io["tril_d"] = nc.dram_tensor("tril_d", [128, 128], F32, kind=EI).ap()
    io["idf32_d"] = nc.dram_tensor("idf32_d", [128, 128], F32, kind=EI).ap()
    io["acol_d"] = nc.dram_tensor("acol_d", [128, 1], F32, kind=EI).ap()
    io["mask_d"] = nc.dram_tensor("mask_d", [128, 1], F32, kind=EI).ap()
    io["normw_d"] = nc.dram_tensor("normw_d", [128, 64], F32, kind=EI).ap()
    io["WoutT"] = nc.dram_tensor("WoutT", [INTER, DM], F32, kind=EI).ap()
    io["w1qT"] = nc.dram_tensor("w1qT", [1024, FH], F32, kind=EI).ap()
    io["w3qT"] = nc.dram_tensor("w3qT", [1024, FH], F32, kind=EI).ap()
    io["w2qT"] = nc.dram_tensor("w2qT", [FQ, DM], F32, kind=EI).ap()
    io["a1T"] = nc.dram_tensor("a1T", [DM, 2], F32, kind=EI).ap()
    io["a3T"] = nc.dram_tensor("a3T", [DM, 2], F32, kind=EI).ap()
    io["a2T"] = nc.dram_tensor("a2T", [FH, 2], F32, kind=EI).ap()
    io["b1T"] = nc.dram_tensor("b1T", [2, FH], F32, kind=EI).ap()
    io["b3T"] = nc.dram_tensor("b3T", [2, FH], F32, kind=EI).ap()
    io["b2T"] = nc.dram_tensor("b2T", [2, DM], F32, kind=EI).ap()
    io["out_d"] = nc.dram_tensor("out_d", [T, DM], F32, kind="ExternalOutput").ap()
    for nm, shp, dt in (("gate_sp", [T, INTER], BF16),
                        ("xtm_sp", [T, INTER], BF16),
                        ("bc_sp", [2048, T], BF16),
                        ("dt_sp", [128, T], F32),
                        ("ypart_sp", [T, INTER], BF16),
                        ("slo_sp", [128, INTER], BF16),
                        ("e3_sp", [NCH, 128, 128], F32),
                        ("expcs_sp", [NCH, 128, 128], F32),
                        ("csfl", [NCH, 16384], F32),
                        ("cdrow_d", [NCH, 128], F32),
                        ("cumrow_d", [NCH, 128], F32),
                        ("sing", [256, INTER], BF16),
                        ("wout_bf", [INTER, DM], BF16),
                        ("w1_bf", [1024, FH], BF16), ("w3_bf", [1024, FH], BF16),
                        ("w2_bf", [FQ, DM], BF16),
                        ("a1_bf", [DM, 2], BF16), ("a3_bf", [DM, 2], BF16),
                        ("a2_bf", [FH, 2], BF16),
                        ("b1_bf", [2, FH], BF16), ("b3_bf", [2, FH], BF16),
                        ("b2_bf", [2, DM], BF16),
                        ("ycorr", [T, INTER], BF16)):
        io[nm] = nc.dram_tensor(nm, shp, dt, kind="Internal").ap()
    io["sin_d"] = io["sing"][0:128, :]

    with ExitStack() as ctx:
        p = Prog()
        p.sem("qs", "qg", "pe", "dve", "act", "pool", "bar", "qcc")
        with ExitStack() as c1:
            sb1 = alloc_phase1_sb(nc, c1)
            ps1 = alloc_phase1_ps(nc, c1)
            emit_phase1(p, io, sb1, ps1)
        p.barrier_all()
        with ExitStack() as c2:
            sbp = alloc_prestage_sb(nc, c2)
            emit_prestage(p, io, sbp)
            sb2 = alloc_phase2_sb(nc, c2)
            ps2 = alloc_phase2_ps(nc, c2)
            emit_phase2(p, io, sb2, ps2)
        p.barrier_all()
        def f_cc(e):
            return e.collective_compute(
                "AllGather", ALU.bypass,
                replica_groups=[[0, 1], [2, 3], [4, 5], [6, 7]],
                ins=[io["slo_sp"]], outs=[io["sing"]])
        p.on("gpsimd", f_cc, inc=[("qcc", 16)])
        p.on("gpsimd", None, wait=[("qcc", 16)])
        p.barrier_all()
        ps3 = alloc_phase3_ps(nc, ctx)
        with ExitStack() as c25:
            sb25 = alloc_phase25_sb(nc, c25)
            q_corr = emit_phase25(p, io, sb25, ps3)
        sb3 = alloc_phase3_sb(nc, ctx)
        emit_phase3(p, io, sb3, ps3, q_corr)
        sems = {n: ctx.enter_context(nc.semaphore(n)) for n in p.cnt}
        block = ctx.enter_context(nc.Block())
        p.install(block, sems)
    return nc


def host_layout_fused(inputs):
    m1 = host_layout_phase1(inputs)
    x0 = np.asarray(inputs["L0_attn_hidden_states"], np.float32)

    def padh(w):
        w = np.asarray(w, np.float32)
        q = w.reshape(4, 2752, -1)
        out = np.zeros((4, FQ, w.shape[1]), np.float32)
        out[:, :2752] = q
        return out.reshape(FH, w.shape[1])

    shared = {
        "drow_d": np.asarray(inputs["L0_attn_D"], np.float32).reshape(1, 128),
        "tril_d": np.triu(np.ones((128, 128), np.float32)),
        "idf32_d": np.eye(128, dtype=np.float32),
        "acol_d": -np.arange(1, 129, dtype=np.float32).reshape(128, 1),
        "normw_d": np.ascontiguousarray(
            np.asarray(inputs["L0_attn_norm_w"], np.float32).reshape(64, 128).T),
        "WoutT": np.ascontiguousarray(
            np.asarray(inputs["L0_attn_out_proj_w"], np.float32).T),
        "w1qT": np.ascontiguousarray(padh(inputs["L0_ffn_w1q"]).T),
        "w3qT": np.ascontiguousarray(padh(inputs["L0_ffn_w3q"]).T),
        "a1T": np.ascontiguousarray(np.asarray(inputs["L0_ffn_a1"], np.float32).T),
        "a3T": np.ascontiguousarray(np.asarray(inputs["L0_ffn_a3"], np.float32).T),
        "a2T": np.ascontiguousarray(padh(np.ascontiguousarray(
            np.asarray(inputs["L0_ffn_a2"], np.float32).T))),
        "b1T": np.ascontiguousarray(padh(inputs["L0_ffn_b1"]).T),
        "b3T": np.ascontiguousarray(padh(inputs["L0_ffn_b3"]).T),
        "b2T": np.ascontiguousarray(np.asarray(inputs["L0_ffn_b2"], np.float32).T),
    }
    w2qT = np.zeros((FQ, DM), np.float32)
    w2qT[:2752] = np.asarray(inputs["L0_ffn_w2q"], np.float32).T
    shared["w2qT"] = w2qT
    maps = []
    for core in range(NCORES):
        s = core % 2
        m = dict(m1[core])
        m.update(shared)
        m["mask_d"] = np.full((128, 1), 1.0 if s == 1 else 0.0, np.float32)
        maps.append(m)
    return maps


def kernel_fused(**inputs):
    cores = list(range(NCORES))
    maps = host_layout_fused(inputs)
    r = run_bass_kernel_spmd(_get_nc("fused", build_fused), maps, core_ids=cores)
    out = np.zeros((BATCH, SEQ, DM), np.float32)
    for core in cores:
        b, s = core // 2, core % 2
        out[b, s * T:(s + 1) * T] = r.results[core]["out_d"]
    return out


def _io_common_a(nc):
    EI = "ExternalInput"
    io = {}
    io["x0c"] = nc.dram_tensor("x0c", [TH, DM], F32, kind=EI).ap()
    io["WinT"] = nc.dram_tensor("WinT", [DM, PROJ], F32, kind=EI).ap()
    io["cw_d"] = nc.dram_tensor("cw_d", [128, 320], F32, kind=EI).ap()
    io["cb_d"] = nc.dram_tensor("cb_d", [128, 80], F32, kind=EI).ap()
    io["dtb_d"] = nc.dram_tensor("dtb_d", [128, 1], F32, kind=EI).ap()
    io["idbf_d"] = nc.dram_tensor("idbf_d", [128, 128], BF16, kind=EI).ap()
    io["drow_d"] = nc.dram_tensor("drow_d", [1, 128], F32, kind=EI).ap()
    io["tril_d"] = nc.dram_tensor("tril_d", [128, 128], F32, kind=EI).ap()
    io["idf32_d"] = nc.dram_tensor("idf32_d", [128, 128], F32, kind=EI).ap()
    io["acol_d"] = nc.dram_tensor("acol_d", [128, 1], F32, kind=EI).ap()
    return io


def build_fused_a():
    nc = bass.Bass("TRN2", target_bir_lowering=False)
    io = _io_common_a(nc)
    EO = "ExternalOutput"
    io["gate_sp"] = nc.dram_tensor("gate_sp", [T, INTER], BF16, kind=EO).ap()
    io["bc_sp"] = nc.dram_tensor("bc_sp", [2048, T], BF16, kind=EO).ap()
    io["ypart_sp"] = nc.dram_tensor("ypart_sp", [T, INTER], BF16, kind=EO).ap()
    io["slo_sp"] = nc.dram_tensor("slo_sp", [128, INTER], BF16, kind=EO).ap()
    io["e3_sp"] = nc.dram_tensor("e3_sp", [NCH, 128, 128], F32, kind=EO).ap()
    for nm, shp, dt in (("xtm_sp", [T, INTER], BF16),
                        ("dt_sp", [128, T], F32),
                        ("expcs_sp", [NCH, 128, 128], F32),
                        ("csfl", [NCH, 16384], F32),
                        ("cdrow_d", [NCH, 128], F32),
                        ("cumrow_d", [NCH, 128], F32)):
        io[nm] = nc.dram_tensor(nm, shp, dt, kind="Internal").ap()
    with ExitStack() as ctx:
        p = Prog()
        p.sem("qs", "qg", "pe", "dve", "act", "pool", "bar")
        with ExitStack() as c1:
            sb1 = alloc_phase1_sb(nc, c1)
            ps1 = alloc_phase1_ps(nc, c1)
            emit_phase1(p, io, sb1, ps1)
        p.barrier_all()
        with ExitStack() as c2:
            sb2 = alloc_phase2_sb(nc, c2)
            ps2 = alloc_phase2_ps(nc, c2)
            emit_phase2(p, io, sb2, ps2)
        sems = {n: ctx.enter_context(nc.semaphore(n)) for n in p.cnt}
        block = ctx.enter_context(nc.Block())
        p.install(block, sems)
    return nc


def build_fused_b():
    nc = bass.Bass("TRN2", target_bir_lowering=False)
    EI = "ExternalInput"
    io = {}
    io["x0c"] = nc.dram_tensor("x0c", [TH, DM], F32, kind=EI).ap()
    io["gate_sp"] = nc.dram_tensor("gate_sp", [T, INTER], BF16, kind=EI).ap()
    io["bc_sp"] = nc.dram_tensor("bc_sp", [2048, T], BF16, kind=EI).ap()
    io["ypart_sp"] = nc.dram_tensor("ypart_sp", [T, INTER], BF16, kind=EI).ap()
    io["sin_d"] = nc.dram_tensor("sin_d", [128, INTER], BF16, kind=EI).ap()
    io["e3_sp"] = nc.dram_tensor("e3_sp", [NCH, 128, 128], F32, kind=EI).ap()
    io["mask_d"] = nc.dram_tensor("mask_d", [128, 1], F32, kind=EI).ap()
    io["idbf_d"] = nc.dram_tensor("idbf_d", [128, 128], BF16, kind=EI).ap()
    io["normw_d"] = nc.dram_tensor("normw_d", [128, 64], F32, kind=EI).ap()
    io["WoutT"] = nc.dram_tensor("WoutT", [INTER, DM], F32, kind=EI).ap()
    io["w1qT"] = nc.dram_tensor("w1qT", [1024, FH], F32, kind=EI).ap()
    io["w3qT"] = nc.dram_tensor("w3qT", [1024, FH], F32, kind=EI).ap()
    io["w2qT"] = nc.dram_tensor("w2qT", [FQ, DM], F32, kind=EI).ap()
    io["a1T"] = nc.dram_tensor("a1T", [DM, 2], F32, kind=EI).ap()
    io["a3T"] = nc.dram_tensor("a3T", [DM, 2], F32, kind=EI).ap()
    io["a2T"] = nc.dram_tensor("a2T", [FH, 2], F32, kind=EI).ap()
    io["b1T"] = nc.dram_tensor("b1T", [2, FH], F32, kind=EI).ap()
    io["b3T"] = nc.dram_tensor("b3T", [2, FH], F32, kind=EI).ap()
    io["b2T"] = nc.dram_tensor("b2T", [2, DM], F32, kind=EI).ap()
    io["out_d"] = nc.dram_tensor("out_d", [T, DM], F32, kind="ExternalOutput").ap()
    for nm, shp, dt in (("wout_bf", [INTER, DM], BF16),
                        ("w1_bf", [1024, FH], BF16), ("w3_bf", [1024, FH], BF16),
                        ("w2_bf", [FQ, DM], BF16),
                        ("a1_bf", [DM, 2], BF16), ("a3_bf", [DM, 2], BF16),
                        ("a2_bf", [FH, 2], BF16),
                        ("b1_bf", [2, FH], BF16), ("b3_bf", [2, FH], BF16),
                        ("b2_bf", [2, DM], BF16),
                        ("ycorr", [T, INTER], BF16)):
        io[nm] = nc.dram_tensor(nm, shp, dt, kind="Internal").ap()
    with ExitStack() as ctx:
        p = Prog()
        p.sem("qs", "qg", "pe", "dve", "act", "pool", "bar")
        ps = alloc_phase3_ps(nc, ctx)
        with ExitStack() as c0:
            sbp = alloc_prestage_sb(nc, c0)
            emit_prestage(p, io, sbp)
        with ExitStack() as c1:
            sb25 = alloc_phase25_sb(nc, c1)
            q_corr = emit_phase25(p, io, sb25, ps)
        sb = alloc_phase3_sb(nc, ctx)
        emit_phase3(p, io, sb, ps, q_corr)
        sems = {n: ctx.enter_context(nc.semaphore(n)) for n in p.cnt}
        block = ctx.enter_context(nc.Block())
        p.install(block, sems)
    return nc


def kernel(**inputs):
    cores = list(range(NCORES))
    mapsA = host_layout_fused(inputs)
    rA = run_bass_kernel_spmd(_get_nc("fa", build_fused_a), mapsA, core_ids=cores)
    aout = rA.results
    mapsB = []
    for core in cores:
        b, s = core // 2, core % 2
        m = {k: mapsA[core][k] for k in
             ("x0c", "mask_d", "idbf_d", "normw_d", "WoutT", "w1qT", "w3qT",
              "w2qT", "a1T", "a3T", "a2T", "b1T", "b3T", "b2T")}
        m["gate_sp"] = aout[core]["gate_sp"]
        m["bc_sp"] = aout[core]["bc_sp"]
        m["ypart_sp"] = aout[core]["ypart_sp"]
        m["e3_sp"] = aout[core]["e3_sp"]
        m["sin_d"] = aout[core - 1]["slo_sp"] if s == 1 else aout[core]["slo_sp"]
        mapsB.append(m)
    rB = run_bass_kernel_spmd(_get_nc("fb", build_fused_b), mapsB, core_ids=cores)
    out = np.zeros((BATCH, SEQ, DM), np.float32)
    for core in cores:
        b, s = core // 2, core % 2
        out[b, s * T:(s + 1) * T] = rB.results[core]["out_d"]
    return out
